# version 24 (final)
# baseline (speedup 1.0000x reference)
import numpy as np
import ml_dtypes
import concourse.bass as bass
import concourse.mybir as mybir
from concourse.bass_utils import run_bass_kernel_spmd

F32 = mybir.dt.float32
BF16 = mybir.dt.bfloat16
I32 = mybir.dt.int32
AF = mybir.ActivationFunctionType
ALU = mybir.AluOpType
AX = mybir.AxisListType.X

H = 1024
HC = 8
T = 2048
NT = 4
S = 256
SEG = 256
NSEG = T // SEG
CC = 128
NCH = T // CC
FF = 4096
FC = 32
LN_EPS = 1e-5

GROUPS = [["l0a", "l0f", "l1a", "l1f"]]
L1A_PH = {"v", "evv", "S", "smt", "inter", "dec", "intra", "rdev", "rdtr",
          "wktr", "upd", "gam", "mem", "ww", "cast"}
RDTR_N = HC
RDTR_COPY = True
RDTR_MODE = "norm"
L1A_FIN = True
PRO_MIN = False

ENGS = ("sync", "tensor", "vector", "scalar", "gpsimd")


class Build:
    def __init__(self, nc):
        self.nc = nc
        self.q = {e: [] for e in ENGS}
        self.sems = {}
        self.cnt = {}
        self.waited = {e: {} for e in ENGS}

    def sem(self, name):
        if name not in self.sems:
            self.sems[name] = self.nc.semaphore(name).__enter__()
            self.cnt[name] = 0
        return self.sems[name]

    def bar(self, name):
        self.sem(name)
        return (name, self.cnt[name])

    def _wait(self, eng, tok):
        if tok is None:
            return
        s, v = tok
        if v <= 0:
            return
        if self.waited[eng].get(s, 0) >= v:
            return
        self.waited[eng][s] = v
        h = self.sem(s)
        self.q[eng].append(lambda e, h=h, v=v: e.wait_ge(h, v))

    SELF_CHAIN = {"vector": "e_v", "scalar": "e_s", "gpsimd": "e_g"}

    def step(self, eng, fn, inc=None, waits=(), chain=True):
        chain = self.SELF_CHAIN.get(eng) if chain else None
        waits = list(waits)
        if chain is not None:
            prev = getattr(self, "_last_" + eng, None)
            if prev is not None:
                waits.append(prev)
            if inc is None:
                inc = chain
        for tk in waits:
            self._wait(eng, tk)
        tok = None
        if inc is not None:
            h = self.sem(inc)
            self.cnt[inc] += 1
            tok = (inc, self.cnt[inc])
            self.q[eng].append(lambda e, fn=fn, h=h: fn(e).then_inc(h, 1))
        else:
            self.q[eng].append(lambda e, fn=fn: fn(e))
        if chain is not None:
            setattr(self, "_last_" + eng, tok)
        return tok

    def dma(self, qeng, out, in_, key, waits=()):
        for tk in waits:
            self._wait(qeng, tk)
        h = self.sem(key)
        self.cnt[key] += 16
        tok = (key, self.cnt[key])
        self.q[qeng].append(
            lambda e, o=out, i=in_, h=h: e.dma_start(out=o, in_=i).then_inc(h, 16))
        return tok

    def replay(self, block):
        for eng in ENGS:
            sec = getattr(block, eng)
            queue = self.q[eng]

            def emit(e, queue=queue):
                for fn in queue:
                    fn(e)
            sec(emit)


def build_group(stages, first, last):
    nc = bass.Bass("TRN2", target_bir_lowering=False)
    d = {}

    def din(name, shape, dt=F32):
        d[name] = nc.dram_tensor(name, shape, dt, kind="ExternalInput").ap()
        return d[name]

    din("xT", [H, T])
    din("bmT", [H, S]); din("aT", [S, S]); din("cT", [S, H]); din("dT", [H, H])
    din("lnw_c", [128, HC]); din("lnb_c", [128, HC])
    din("w1T", [H, FF]); din("b1_c", [128, FC]); din("w2T", [FF, H]); din("b2_c", [128, HC])
    din("l0wT", [H, H])
    din("wwT", [H, H]); din("wrT", [H, H]); din("decay", [1, 1]); din("logalpha", [1, 1])
    din("wfcT", [H, FF]); din("bfc_c", [128, FC])
    din("afcT", [H, 16]); din("bfcT", [16, FF])
    din("wprojT", [FF, H]); din("bprj_c", [128, HC])
    din("aprojT", [FF, 16]); din("bprojT", [16, H])
    din("identb", [128, 128], BF16); din("identf", [128, 128])
    din("onesb_col", [128, 1], BF16); din("onesf_row", [1, 128])
    din("arange", [128, 1]); din("rev", [128, 1]); din("diffs", [128, 128])
    if not first:
        din("sin", [H, T])
    if last:
        y = nc.dram_tensor("y", [H, T], F32, kind="ExternalOutput").ap()
    else:
        y = nc.dram_tensor("sout", [H, T], F32, kind="ExternalOutput").ap()
    wqfc = nc.dram_tensor("wqfc", [H, FF], BF16, kind="Internal").ap()
    wqproj = nc.dram_tensor("wqproj", [FF, H], BF16, kind="Internal").ap()
    l0wqb = nc.dram_tensor("l0wqb", [H, H], BF16, kind="Internal").ap()
    w1b = nc.dram_tensor("w1b", [H, FF], BF16, kind="Internal").ap()
    w2b = nc.dram_tensor("w2b", [FF, H], BF16, kind="Internal").ap()
    xdb = nc.dram_tensor("xdb", [128, HC * T], BF16, kind="Internal").ap()

    import contextlib
    st = contextlib.ExitStack()

    def sb(name, shape, dt=F32):
        return st.enter_context(nc.sbuf_tensor(name, shape, dt))

    with st:
        AR0 = sb("AR0", [128, HC, T])
        AR1 = sb("AR1", [128, HC, 1024])
        ARB = sb("ARB", [128, 24576], BF16)
        XBT = sb("XBT", [128, HC, 512], BF16)
        STT = sb("STT", [128, T + 1, 2], BF16)
        ATB = sb("ATB", [128, 2, S], BF16)
        CTB = sb("CTB", [128, 2, H], BF16)
        WWB = sb("WWB", [128, HC, H], BF16)
        WSTB = sb("WSTB", [128, HC, H], BF16)
        COLS = sb("COLS", [128, 16])
        CELLS = sb("CELLS", [1, 16])
        LNW = sb("LNW", [128, HC]); LNB = sb("LNB", [128, HC])
        B2C = sb("B2C", [128, HC]); BPJ = sb("BPJ", [128, HC])
        B1C = sb("B1C", [128, FC]); BFCc = sb("BFCc", [128, FC])
        MSK = sb("MSK", [128, 128])
        IDB = sb("IDB", [128, 128], BF16); IDF = sb("IDF", [128, 128])
        ONBC = sb("ONBC", [128, 1], BF16); ONFR = sb("ONFR", [1, 128])
        ONFC = sb("ONFC", [128, 1])
        ARNG = sb("ARNG", [128, 1]); REVC = sb("REVC", [128, 1])
        AFCT = sb("AFCT", [128, HC, 16], BF16)
        BFCT = sb("BFCT", [16, FF], BF16)

        P = [st.enter_context(nc.psum_tensor(f"PB{i}", [128, 512], F32))
             for i in range(8)]

        hm = ARB[:, 0:16384].rearrange("p (m t) -> p m t", m=FC)
        reads = ARB[:, 0:16384].rearrange("p (k t) -> p k t", k=HC)
        l0wq = ARB[:, 0:8192].rearrange("p (k o) -> p k o", k=HC)
        WS = [ARB[:, 16384:17408].rearrange("p (k c) -> p k c", k=HC),
              ARB[:, 17408:18432].rearrange("p (k c) -> p k c", k=HC)]
        SC = ARB[:, 18432:24576]
        rbq = SC[:, 0:1024].rearrange("p (b t) -> p b t", b=2)
        sqq = SC[:, 1024:2048].rearrange("p (b t) -> p b t", b=2)
        wpq = SC[:, 0:4096].rearrange("p (b c) -> p b c", b=2)
        vtm = SC[:, 2048:4096].rearrange("p (a c) -> p a c", a=2)
        utm = SC[:, 4096:6144].rearrange("p (a c) -> p a c", a=2)
        wkt = SC[:, 2048:4096]
        vtm = SC[:, 2048:3072].rearrange("p (a c) -> p a c", a=2)
        utm = SC[:, 3072:4096].rearrange("p (a c) -> p a c", a=2)
        wkt = SC[:, 4096:5120]
        rtm = SC[:, 5120:6144]
        x2c = XBT[:, :, 0:260].rearrange("p k (b c) -> p b k c", b=2)
        OUTB = SC[:, 2048:4096].bitcast(F32)
        ROWS = SC[0:1, 2048:6144].bitcast(F32)
        BCS = SC[:, 4096:5120].bitcast(F32)
        T1F = SC[:, 5120:6144].bitcast(F32)
        I32S = SC[:, 5120:6144].bitcast(I32)
        BMT = WSTB[:, 1:3, :].rearrange(
            "p a b -> p (a b)").rearrange("p (k s) -> p k s", k=HC)
        XDS = [ARB[:, 0:256], ARB[:, 256:512]]
        DTR = ARB[:, 512:8704].rearrange("p (k n) -> p k n", k=HC)
        WSF = [ARB[:, 16384:16896], ARB[:, 17408:17920]]
        STTF = STT[:, :, :].rearrange("p a b -> p (a b)")
        BPJT = STTF[0:16, 0:1024]
        XAB = STTF[0:16, 1024:1536]
        HAB = STTF[0:16, 1536:2048]
        APJT = STTF[:, 2048:2560].rearrange("p (m r) -> p m r", m=FC)
        SMB = STTF[:, 2560:2688]
        SMTB = STTF[:, 2688:2816]
        WS.append(WSTB[:, 0, :].rearrange("p (k c) -> p k c", k=HC))

        B = Build(nc)
        RSET = [(ROWS[:, 0:512], ROWS[:, 512:1024], ROWS[:, 1024:1536],
                 ROWS[:, 1536:2048])] * 2

        def r3(ap, k=128):
            return ap.rearrange("(kc k) n -> k kc n", k=k)

        QFLG = SC[:, 4096:6144]
        I32Q = SC[:, 4096:5120].bitcast(I32)

        def emit_round(tview, qout, rinv_col, s_col, lo, hi, waits):
            B.step("vector", lambda e: e.tensor_scalar(
                tview, tview, rinv_col, 0.0, op0=ALU.mult, op1=ALU.add),
                inc=None, waits=list(waits))
            for off in range(0, 2048, 512):
                B.step("vector", lambda e, off=off: e.tensor_copy(
                    I32Q[:, :], tview[:, off:off + 512]), inc=None, waits=[])
                B.step("vector", lambda e, off=off: e.tensor_copy(
                    qout[:, off:off + 512], I32Q[:, :]), inc=None, waits=[])
            B.step("vector", lambda e: e.tensor_tensor(
                tview, tview, qout, op=ALU.subtract), inc=None, waits=[])
            B.step("vector", lambda e: e.tensor_scalar(
                QFLG, tview, 0.5, 0.0, op0=ALU.is_ge, op1=ALU.add),
                inc=None, waits=[])
            B.step("vector", lambda e: e.tensor_tensor(
                qout, qout, QFLG, op=ALU.add), inc=None, waits=[])
            B.step("vector", lambda e: e.tensor_scalar(
                QFLG, tview, -0.5, 0.0, op0=ALU.is_le, op1=ALU.add),
                inc=None, waits=[])
            B.step("vector", lambda e: e.tensor_tensor(
                qout, qout, QFLG, op=ALU.subtract), inc=None, waits=[])
            B.step("vector", lambda e, lo=lo, hi=hi: e.tensor_scalar(
                qout, qout, float(lo), float(hi), op0=ALU.max, op1=ALU.min),
                inc=None, waits=[])
            return B.step("vector", lambda e: e.tensor_scalar(
                qout, qout, s_col, 0.0, op0=ALU.mult, op1=ALU.add),
                inc="e_v", waits=[])

        _xsrc = d["xT"] if first else d["sin"]
        tok_xs01 = [B.dma("sync", AR0[:, hc, 0:512],
                          _xsrc[hc * 128:(hc + 1) * 128, 0:512], f"d_x{hc}")
                    for hc in range(HC)]
        tok_xs = [B.dma("sync", AR0[:, hc, 512:T],
                        _xsrc[hc * 128:(hc + 1) * 128, 512:T], f"d_x{hc}")
                  for hc in range(HC)]
        tok_x = tok_xs[-1]
        _pro = [("identf", IDF[:, :]), ("onesf_row", ONFR[:, :]),
                ("lnw_c", LNW[:, :]), ("lnb_c", LNB[:, :]),
                ("b2_c", B2C[:, :]), ("b1_c", B1C[:, :]),
                ("bprj_c", BPJ[:, :]), ("bfc_c", BFCc[:, :]),
                ("arange", ARNG[:, :]), ("rev", REVC[:, :]),
                ("diffs", MSK[:, :])]
        if PRO_MIN:
            _pro = _pro[:1]
        for nm, dst in _pro:
            B.dma("sync", dst, d[nm][:, :], "d_pro")
        B.dma("sync", CELLS[0:1, 0:1], d["decay"][:, :], "d_pro")
        tok_pro = B.dma("sync", CELLS[0:1, 1:2], d["logalpha"][:, :], "d_pro")
        tok_idb = B.dma("gpsimd", IDB[:, :], d["identb"][:, :], "d_idb")
        if PRO_MIN:
            tok_onb = tok_bmt = tok_atb = tok_ctb = tok_afc = tok_bfct = None
        else:
            tok_onb = B.dma("gpsimd", ONBC[:, :], d["onesb_col"][:, :], "d_onb")
            tok_bmt = (B.dma("gpsimd", BMT[:, :, :], r3(d["bmT"]), "d_bmt")
                       if "l0a" in stages else None)
            tok_atb = B.dma("gpsimd", ATB[:, :, :], r3(d["aT"]), "d_atb")
            tok_ctb = B.dma("gpsimd", CTB[:, :, :], r3(d["cT"]), "d_ctb")
            tok_afc = B.dma("gpsimd", AFCT[:, :, :], r3(d["afcT"]), "d_afc")
            tok_bfct = B.dma("gpsimd", BFCT[:, :], d["bfcT"][:, :], "d_bfct")
        tok_ones = B.step("vector", lambda e: e.memset(ONFC[:, :], 1.0), inc="e_v")
        tok_sttz = B.step("vector", lambda e: e.memset(
            STT[:, 0:1, :].rearrange("p a b -> p (a b)"), 0.0), inc="e_v")

        B.p4_free = None
        B.p5_free = None
        B.p67_free = None
        B.last_stats = None
        B.ws_free = [None, None, None]
        B.qprep_done = (None, None)
        B.resid_hc = {}

        def bcast_row(row_ap, bank, waits):
            return B.step("tensor",
                          lambda e, r=row_ap, b=bank: e.matmul(
                              b[:, :], ONFR[0:1, :], r, start=True, stop=True,
                              skip_group_check=True),
                          inc="e_t", waits=list(waits) + [tok_pro])

        def ln_collect_start():
            B.stats_k = 0
            B.stats_last = {0: None, 1: None}
            B.stats_pending = []

        def ln_collect(rb_ap, sq_ap, waits_rb, waits_sq):
            k = B.stats_k
            w1 = list(waits_rb) + [tok_onb]
            if k == 0:
                w1.append(B.p67_free)
            B.step("tensor", lambda e, r=rb_ap: e.matmul(
                P[6][0:1, :], ONBC[:, :], r, start=(k == 0), stop=(k == HC - 1),
                skip_group_check=True), inc=None, waits=w1)
            t2 = B.step("tensor", lambda e, q2=sq_ap: e.matmul(
                P[7][0:1, :], ONBC[:, :], q2, start=(k == 0), stop=(k == HC - 1),
                skip_group_check=True), inc="e_t", waits=list(waits_sq))
            B.stats_k += 1
            B.last_stats = t2

        def ln_rows(eps=LN_EPS, alt=0):
            ROW_MU, ROW_RS, ROW_MR, ROW_TMP = RSET[alt]
            while B.stats_pending:
                p2, rb2, sq2 = B.stats_pending.pop(0)
                ln_collect(rbq[:, p2, :], sqq[:, p2, :], [rb2], [sq2])
                B.stats_last[p2] = B.last_stats
            t_mu = B.step("vector", lambda e: e.tensor_scalar(
                ROW_MU[:, :], P[6][0:1, :], 1.0 / H, 0.0, op0=ALU.mult, op1=ALU.add),
                inc="e_v", waits=[B.last_stats])
            t_ms = B.step("vector", lambda e: e.tensor_scalar(
                ROW_TMP[:, :], P[7][0:1, :], 1.0 / H, 0.0, op0=ALU.mult, op1=ALU.add),
                inc="e_v", waits=[])
            B.p67_free = t_ms
            t_rs = B.step("vector", lambda e: e.scalar_tensor_tensor(
                ROW_RS[:, :], ROW_MU[:, :], -1.0, ROW_MU[:, :],
                op0=ALU.mult, op1=ALU.mult), inc="e_v", waits=[t_mu])
            t_tmp = B.step("vector", lambda e: e.tensor_tensor(
                ROW_TMP[:, :], ROW_TMP[:, :], ROW_RS[:, :], op=ALU.add),
                inc="e_v", waits=[t_rs, t_ms])
            t_eps = B.step("vector", lambda e, eps=eps: e.tensor_scalar(
                ROW_TMP[:, :], ROW_TMP[:, :], eps, 0.0, op0=ALU.add, op1=ALU.add),
                inc="e_v", waits=[t_tmp])
            ts = B.step("scalar", lambda e: e.activation(
                ROW_RS[:, :], ROW_TMP[:, :], AF.Ln), inc="e_s", waits=[t_eps])
            ts = B.step("scalar", lambda e: e.activation(
                ROW_RS[:, :], ROW_RS[:, :], AF.Exp, scale=-0.5), inc="e_s", waits=[ts])
            tv2 = B.step("vector", lambda e: e.scalar_tensor_tensor(
                ROW_MR[:, :], ROW_MU[:, :], -1.0, ROW_RS[:, :],
                op0=ALU.mult, op1=ALU.mult), inc="e_v", waits=[ts, t_mu])
            return tv2, ts

        def ln_apply(dst, tok_rows_v, tok_rows_s, affine=False, alt=0,
                     tau=None):
            ROW_MU, ROW_RS, ROW_MR, ROW_TMP = RSET[alt]
            trs = bcast_row(ROW_RS[0:1, :], P[4],
                            [tok_rows_v, tok_rows_s, B.p4_free])
            tmr = bcast_row(ROW_MR[0:1, :], P[5], [tok_rows_v, B.p5_free])
            tb = B.step("vector", lambda e: e.tensor_copy(BCS[:, :], P[5][:, :]),
                        inc="e_v", waits=[tmr])
            tv = None
            hc_toks = []
            for hc in range(HC):
                B.step("vector", lambda e, hc=hc: e.tensor_tensor(
                    dst[:, hc, :], dst[:, hc, :], P[4][:, :], op=ALU.mult),
                    inc=None, waits=([trs] if hc == 0 else []))
                if affine:
                    B.step("vector", lambda e, hc=hc: e.tensor_tensor(
                        dst[:, hc, :], dst[:, hc, :], BCS[:, :], op=ALU.add),
                        inc=None, waits=([tb] if hc == 0 else []))
                    tv = B.step("vector", lambda e, hc=hc: e.tensor_scalar(
                        dst[:, hc, :], dst[:, hc, :], LNW[:, hc:hc + 1],
                        LNB[:, hc:hc + 1], op0=ALU.mult, op1=ALU.add),
                        inc="e_v", waits=[])
                else:
                    tv = B.step("vector", lambda e, hc=hc: e.tensor_tensor(
                        dst[:, hc, :], dst[:, hc, :], BCS[:, :], op=ALU.add),
                        inc="e_v", waits=([tb] if hc == 0 else []))
                hc_toks.append(tv)
            B.apply_hc = hc_toks
            if tau is not None:
                B.resid_hc[tau] = hc_toks
            B.p4_free = tv
            B.p5_free = tv
            return tv

        def cast_tile(dst, src, waits, per_hc=None):
            toks = []
            tk = None
            for hc in range(HC):
                w = list(waits)
                if per_hc is not None:
                    w.append(per_hc[hc])
                if hc % 2 == 0:
                    tk = B.step("vector", lambda e, hc=hc, d2=dst, s2=src:
                                e.tensor_copy(d2[:, hc, :], s2[:, hc, :]),
                                inc="e_v", waits=w)
                else:
                    tk = B.step("scalar", lambda e, hc=hc, d2=dst, s2=src:
                                e.copy(d2[:, hc, :], s2[:, hc, :]),
                                inc="e_s", waits=w)
                toks.append(tk)
            B.cast_hc = toks
            return tk

        def collect_std(hc, src_ap, tok_src, flush=False):
            par = hc % 2
            trb = B.step("vector", lambda e, s2=src_ap, par=par: e.tensor_copy(
                rbq[:, par, :], s2), inc="e_v",
                waits=[tok_src, B.stats_last[par]])
            tsq = B.step("scalar", lambda e, s2=src_ap, par=par: e.activation(
                sqq[:, par, :], s2, AF.Square), inc="e_s",
                waits=[tok_src, B.stats_last[par]])
            B.stats_pending.append((par, trb, tsq))
            if len(B.stats_pending) > 1:
                p2, rb2, sq2 = B.stats_pending.pop(0)
                ln_collect(rbq[:, p2, :], sqq[:, p2, :], [rb2], [sq2])
                B.stats_last[p2] = B.last_stats
            if flush:
                while B.stats_pending:
                    p2, rb2, sq2 = B.stats_pending.pop(0)
                    ln_collect(rbq[:, p2, :], sqq[:, p2, :], [rb2], [sq2])
                    B.stats_last[p2] = B.last_stats

        def weight_prepass():
            for hc in range(HC):
                tok_dtr = B.dma(
                    "gpsimd", DTR[:, :, hc * 128:(hc + 1) * 128],
                    r3(d["dT"][:, hc * 128:(hc + 1) * 128]), "d_dtr")
            B.dtr_toks = [tok_dtr] * HC
            jobs = []
            for m in range(FC):
                jobs.append((r3(d["w1T"][:, m * 128:(m + 1) * 128]),
                             r3(w1b[:, m * 128:(m + 1) * 128])))
            for k in range(FC):
                sl = d["w2T"][k * 128:(k + 1) * 128, :]
                s2 = w2b[k * 128:(k + 1) * 128, :]
                jobs.append((sl.rearrange("k (a b) -> k a b", a=HC),
                             s2.rearrange("k (a b) -> k a b", a=HC)))
            stok = [None, None]
            for j, (srcv, dstv) in enumerate(jobs):
                bb = j % 2
                tld = B.dma("gpsimd", WS[bb][:, :, :], srcv, f"d_pw{bb}",
                            waits=[stok[bb]] + ([tok_x] if j < 2 else []))
                stok[bb] = B.dma("sync", dstv, WS[bb][:, :, :], f"d_ps{bb}",
                                 waits=[tld])
            B.prepass_done = tuple(stok)

        def stage_l0a(mid_hook=None):
            seg_free = [None, None]
            xb_done = [None, None]
            tmemset = tok_sttz
            xd_evac = {}
            xds_free = [None, None]
            B.xd_store_seg = {}
            B.xd_done = None
            B.xd_store = None
            cast_tok = [None] * NSEG

            def emit_cast(s2, hc):
                par2 = s2 % 2
                half2 = slice(par2 * 256, par2 * 256 + 256)
                tok_src = tok_xs01[hc] if s2 < 2 else tok_xs[hc]
                cast_tok[s2] = B.step(
                    "scalar", lambda e, hc=hc, s2=s2, half2=half2:
                    e.copy(XBT[:, hc, half2],
                           AR0[:, hc, s2 * SEG:(s2 + 1) * SEG]),
                    inc="e_s",
                    waits=[tok_src]
                    + ([xb_done[par2]] if hc == 0 else []))
            for s in range(NSEG):
                if s == 4 and mid_hook is not None:
                    mid_hook()
                par = s % 2
                half = slice(par * 256, par * 256 + 256)
                if s == 0:
                    for hc in range(HC):
                        emit_cast(0, hc)
                tcast = cast_tok[s]
                tmm = None
                for m in range(2):
                    for k in range(HC):
                        lastk = (m == 1 and k == HC - 1)
                        tmm = B.step("tensor", lambda e, m=m, k=k, par=par,
                                     half=half:
                                     e.matmul(P[par][:, m * 256:(m + 1) * 256],
                                              BMT[:, k, m * 128:(m + 1) * 128],
                                              XBT[:, k, half],
                                              start=(m == 0 and k == 0),
                                              stop=(k == HC - 1),
                                              skip_group_check=True),
                                     inc=("e_t" if lastk else None),
                                     waits=([tcast, tok_bmt, tok_atb, seg_free[par],
                                             tmemset] if (m == 0 and k == 0) else []))
                jobs = []
                for hc in range(HC):
                    for k in range(HC):
                        jobs.append((hc, k))
                njobs = len(jobs)
                step_of_job = {}
                first_step = 64 if s == 0 else 0
                avail = SEG - first_step
                for ji in range(njobs):
                    step_of_job.setdefault(first_step + (ji * avail) // njobs,
                                           []).append(ji)
                last_xd = None
                for t in range(SEG):
                    g = s * SEG + t
                    w = [("sca", g)] if g > 0 else []
                    for j in range(2):
                        for kc in range(2):
                            lastmm = (j == 1 and kc == 1)
                            B.step("tensor",
                                   lambda e, j=j, kc=kc, t=t, g=g, par=par:
                                   e.matmul(P[par][:, j * 256 + t:
                                                   j * 256 + t + 1],
                                            ATB[:, kc, j * 128:(j + 1) * 128],
                                            STT[:, g, kc:kc + 1],
                                            start=False, stop=(kc == 1),
                                            skip_group_check=True),
                                   inc=("scm" if lastmm else None),
                                   waits=(w if (j == 0 and kc == 0) else []))
                    for ji in step_of_job.get(t, []):
                        hc, k = jobs[ji]
                        bank = P[2 + hc % 4]
                        wx = []
                        if k == 0:
                            wx = [B.dtr_toks[hc], xd_evac.get((s, hc - 4)),
                                  xd_evac.get((s - 1, hc + 4))]
                        tj = B.step("tensor", lambda e, hc=hc, k=k, par=par,
                                    bank=bank, half=half:
                                    e.matmul(bank[:, 0:256],
                                             DTR[:, k, hc * 128:(hc + 1) * 128],
                                             XBT[:, k, half],
                                             start=(k == 0), stop=(k == HC - 1),
                                             skip_group_check=True),
                                    inc=("e_t" if k == HC - 1 else None),
                                    waits=wx)
                        if k == HC - 1:
                            last_xd = tj
                            slot = (s * HC + hc) % 2
                            tev = B.step(
                                "scalar", lambda e, slot=slot, bank=bank:
                                e.copy(XDS[slot], bank[:, 0:256]),
                                inc="e_s", waits=[tj, xds_free[slot]])
                            xd_evac[(s, hc)] = tev
                            xds_free[slot] = B.dma(
                                "scalar",
                                xdb[:, hc * T + s * SEG:hc * T + (s + 1) * SEG],
                                XDS[slot], "d_xo" + str(slot), waits=[tev])
                            B.xd_done = tev
                            B.xd_store = list(xds_free)
                    if s + 1 < NSEG and t >= 128 and t < 128 + 8 * 8 \
                            and (t - 128) % 8 == 0:
                        emit_cast(s + 1, (t - 128) // 8)
                    tact = B.step("scalar", lambda e, t=t, g=g, par=par: e.activation(
                        STT[:, g + 1, :],
                        P[par][:, :].rearrange("p (h t) -> p t h", h=2)[:, t, :],
                        AF.Tanh),
                        inc="sca", waits=[("scm", g + 1)], chain=False)
                    if t == SEG - 1:
                        seg_free[par] = tact
                if last_xd is not None:
                    xb_done[par] = last_xd
                else:
                    xb_done[par] = tmm
                B.xd_store_seg[s] = list(xds_free)
            tok_scan = ("sca", B.cnt["sca"])

            B.p4_free = B.xd_done
            B.p5_free = B.xd_done
            xd_tok = [None, None]
            pend2 = None
            for tau in range(NT):
                tsl = slice(tau * 512, (tau + 1) * 512)
                papply = pend2() if pend2 is not None else None
                ln_collect_start()
                evac = {}
                for hc in range(HC):
                    if hc == 2 and papply is not None:
                        papply()
                        papply = None
                    par = hc % 2
                    txd = B.dma("sync", WSF[par],
                                xdb[:, hc * T + tau * 512:
                                    hc * T + (tau + 1) * 512],
                                "d_xd" + str(par),
                                waits=[xd_tok[par], B.ws_free[par],
                                       B.prepass_done[0], B.prepass_done[1]]
                                + list(B.xd_store_seg[2 * tau + 1]))
                    bank = P[par]
                    w0 = [txd, evac.get(hc - 2)]
                    if tau == 0 and hc == 0:
                        w0 += [tok_scan, tok_idb] + tok_xs + list(B.qprep_done)
                    tmm = B.step("tensor", lambda e, par=par, bank=bank:
                                 e.matmul(bank[:, :], IDB[:, :], WSF[par],
                                          start=True, stop=False,
                                          skip_group_check=True),
                                 inc=None, waits=w0)
                    for kc in range(2):
                        tmm = B.step("tensor", lambda e, kc=kc, hc=hc, bank=bank,
                                     tau=tau:
                                     e.matmul(bank[:, :],
                                              CTB[:, kc, hc * 128:(hc + 1) * 128],
                                              STT[:, 1 + tau * 512:
                                                  1 + (tau + 1) * 512, kc],
                                              start=False, stop=(kc == 1),
                                              skip_group_check=True),
                                     inc=("e_t" if kc == 1 else None),
                                     waits=([tok_ctb] if kc == 0 else []))
                    xd_tok[par] = tmm
                    B.ws_free[par] = tmm
                    tev = B.step("vector", lambda e, hc=hc, bank=bank, tsl=tsl:
                                 e.tensor_tensor(AR1[:, hc, 0:512], bank[:, :],
                                                 AR0[:, hc, tsl], op=ALU.add),
                                 inc="e_v", waits=[tmm])
                    evac[hc] = tev
                    collect_std(hc, AR1[:, hc, 0:512], tev)
                trv, trs = ln_rows()
                ty = ln_apply(AR1[:, :, 0:512], trv, trs, affine=True)
                tyc = cast_tile(XBT, AR1[:, :, 0:512],
                                [("e_t", B.cnt["e_t"])], per_hc=B.apply_hc)
                cast_hc = B.cast_hc
                gelu_toks = {}
                wtoks = {}
                for m in range(2):
                    wtoks[m] = B.dma("sync", WS[m % 3][:, :, :],
                                     r3(w1b[:, m * 128:(m + 1) * 128]),
                                     "d_w1" + str(m % 3), waits=[B.ws_free[m % 3]])
                for m in range(FC):
                    par = m % 3
                    if m + 2 < FC:
                        wtoks[m + 2] = B.dma(
                            "sync", WS[(m + 2) % 3][:, :, :],
                            r3(w1b[:, (m + 2) * 128:(m + 3) * 128]),
                            "d_w1" + str((m + 2) % 3),
                            waits=[B.ws_free[(m + 2) % 3]])
                    bank = P[m % 2]
                    tmm = None
                    for k in range(HC):
                        if m == 0:
                            w0 = ([cast_hc[0], wtoks[m]] if k == 0
                                  else [cast_hc[k]])
                        else:
                            w0 = ([wtoks[m], gelu_toks.get(m - 2)]
                                  if k == 0 else [])
                        tmm = B.step("tensor", lambda e, k=k, par=par, bank=bank:
                                     e.matmul(bank[:, :], WS[par][:, k, :],
                                              XBT[:, k, :],
                                              start=(k == 0), stop=(k == HC - 1),
                                              skip_group_check=True),
                                     inc=("e_t" if k == HC - 1 else None),
                                     waits=w0)
                    B.ws_free[par] = tmm
                    tg = B.step("scalar", lambda e, m=m, bank=bank: e.activation(
                        hm[:, m, :], bank[:, :], AF.Gelu, bias=B1C[:, m:m + 1]),
                        inc="e_s", waits=[tmm, tok_pro]
                        + list(getattr(B, "xd_store", None) or ()))
                    gelu_toks[m] = tg
                mm2_last = None
                wtoks = {}
                for k in range(2):
                    wtoks[k] = B.dma("sync", WS[k % 3][:, :, :],
                                     w2b[k * 128:(k + 1) * 128, :].rearrange(
                                         "k (a b) -> k a b", a=HC),
                                     "d_w2" + str(k % 3), waits=[B.ws_free[k % 3]])
                for k in range(FC):
                    par = k % 3
                    if k + 2 < FC:
                        wtoks[k + 2] = B.dma(
                            "sync", WS[(k + 2) % 3][:, :, :],
                            w2b[(k + 2) * 128:(k + 3) * 128, :].rearrange(
                                "k (a b) -> k a b", a=HC),
                            "d_w2" + str((k + 2) % 3),
                            waits=[B.ws_free[(k + 2) % 3]])
                    for hc in range(HC):
                        w0 = []
                        if k == 0 and hc == 0:
                            w0 = [wtoks[k], gelu_toks[FC - 1], gelu_toks[FC - 2],
                                  B.bar("e_v"), B.bar("e_s")]
                        elif hc == 0:
                            w0 = [wtoks[k]]
                        mm2_last = B.step("tensor",
                                          lambda e, k=k, hc=hc, par=par: e.matmul(
                                              P[hc][:, :], WS[par][:, hc, :],
                                              hm[:, k, :],
                                              start=(k == 0), stop=(k == FC - 1),
                                              skip_group_check=True),
                                          inc=("e_t" if hc == HC - 1 else None),
                                          waits=w0)
                    B.ws_free[par] = mm2_last
                if papply is not None:
                    papply()
                ln_collect_start()
                ev = {}

                def evac_l0a(hc, tsl=tsl):
                    B.step("vector", lambda e, hc=hc: e.scalar_tensor_tensor(
                        BCS[:, :], P[hc][:, :], B2C[:, hc:hc + 1], AR1[:, hc, 0:512],
                        op0=ALU.add, op1=ALU.add), inc="e_v",
                        waits=[mm2_last])
                    return B.step("vector", lambda e, hc=hc, tsl=tsl:
                                  e.tensor_tensor(AR0[:, hc, tsl], AR0[:, hc, tsl],
                                                  BCS[:, :], op=ALU.add),
                                  inc="e_v", waits=[])
                for hc in (6, 7):
                    ev[hc] = evac_l0a(hc)
                B.p67_free = ev[7]
                for hc in (6, 7, 0, 1, 2, 3, 4, 5):
                    if hc not in ev:
                        ev[hc] = evac_l0a(hc)
                    collect_std(hc, AR0[:, hc, tsl], ev[hc])

                def pend2(tsl=tsl, tau=tau):
                    trv, trs = ln_rows(alt=1)
                    return lambda: ln_apply(AR0[:, :, tsl], trv, trs,
                                            affine=False, alt=1, tau=tau)
            pend2()()

        def stage_l0f():
            B.l0f_bars = (B.bar("e_v"), B.bar("e_t"), B.bar("e_s"))
            wv = [l0wqb[:, hc * 128:(hc + 1) * 128].rearrange(
                "(kc k) o -> k kc o", k=128) for hc in range(HC)]
            pend = None
            for tau in range(NT):
                tsl = slice(tau * 512, (tau + 1) * 512)
                txc = cast_tile(XBT, AR0[:, :, tsl],
                                [B.bar("e_t")] + tok_xs,
                                per_hc=B.resid_hc[tau])
                cast_hc = B.cast_hc
                papply = pend() if pend is not None else None
                ln_collect_start()
                evac = {}
                wtoks = {}
                extra = list(B.qprep_done) if tau == 0 else []
                for hc in range(2):
                    wtoks[hc] = B.dma("sync", WS[hc % 3][:, :, :], wv[hc],
                                      "d_w0" + str(hc % 3),
                                      waits=[B.ws_free[hc % 3]] + extra)
                for hc in range(HC):
                    if hc == 2 and papply is not None:
                        papply()
                        papply = None
                    if hc + 2 < HC:
                        wtoks[hc + 2] = B.dma(
                            "sync", WS[(hc + 2) % 3][:, :, :], wv[hc + 2],
                            "d_w0" + str((hc + 2) % 3),
                            waits=[B.ws_free[(hc + 2) % 3]] + extra)
                    par = hc % 3
                    bank = P[hc % 2]
                    mmt = None
                    for k in range(HC):
                        if hc == 0:
                            w0 = ([cast_hc[0], wtoks[hc]] if k == 0
                                  else [cast_hc[k]])
                        else:
                            w0 = ([wtoks[hc], evac.get(hc - 2)]
                                  if k == 0 else [])
                        mmt = B.step("tensor", lambda e, k=k, par=par, bank=bank:
                                     e.matmul(bank[:, :],
                                              WS[par][:, k, :],
                                              XBT[:, k, :],
                                              start=(k == 0), stop=(k == HC - 1),
                                              skip_group_check=True),
                                     inc=("e_t" if k == HC - 1 else None),
                                     waits=w0)
                    B.ws_free[par] = mmt
                    tev = B.step("vector", lambda e, hc=hc, tsl=tsl:
                                 e.tensor_tensor(AR0[:, hc, tsl], AR0[:, hc, tsl],
                                                 P[hc % 2][:, :], op=ALU.add),
                                 inc="e_v", waits=[mmt])
                    evac[hc] = tev
                    collect_std(hc, AR0[:, hc, tsl], tev)

                def pend(tsl=tsl, tau=tau):
                    trv, trs = ln_rows(alt=1)
                    return lambda: ln_apply(AR0[:, :, tsl], trv, trs,
                                            affine=False, alt=1, tau=tau)
            pend()()

        def stage_l1a():
            bar_v = B.bar("e_v"); bar_s = B.bar("e_s")
            if "gam" not in L1A_PH:
                tlg = tal = tlgc = tac = tdec = tgw = tgc = tmsk = None
                tok_ww = tz1 = tz2 = None
                if "mem" in L1A_PH:
                    tz1 = B.step("vector", lambda e: e.memset(
                        AR1[:, :, :].rearrange("p k o -> p (k o)"), 0.0),
                        inc="e_v", waits=[bar_s])
                    tz2 = B.step("vector", lambda e: e.memset(
                        WSTB[:, :, :].rearrange("p k o -> p (k o)"), 0.0),
                        inc="e_v", waits=[])
                if "ww" in L1A_PH:
                    tok_ww = B.dma("gpsimd", WWB[:, :, :], r3(d["wwT"]), "d_ww")
                return _l1a_body(tlg, tal, tlgc, tac, tdec, tgw, tgc, tmsk,
                                 tok_ww, tz1, tz2)
            tg = B.step("scalar", lambda e: e.activation(
                CELLS[0:1, 4:5], CELLS[0:1, 0:1], AF.Sigmoid), inc="e_s",
                waits=[tok_pro])
            tlg = B.step("scalar", lambda e: e.activation(
                CELLS[0:1, 5:6], CELLS[0:1, 4:5], AF.Ln), inc="e_s", waits=[])
            tal = B.step("scalar", lambda e: e.activation(
                CELLS[0:1, 6:7], CELLS[0:1, 1:2], AF.Exp), inc="e_s", waits=[])
            tb1 = B.step("tensor", lambda e: e.matmul(
                P[6][:, 0:1], ONFR[0:1, :], CELLS[0:1, 5:6], start=True, stop=True,
                skip_group_check=True), inc=None,
                waits=[tlg, tok_pro, B.p67_free, bar_v])
            tb2 = B.step("tensor", lambda e: e.matmul(
                P[6][:, 1:2], ONFR[0:1, :], CELLS[0:1, 6:7], start=False, stop=True,
                skip_group_check=True), inc="e_t", waits=[tal])
            tlgc = B.step("vector", lambda e: e.tensor_copy(COLS[:, 4:5], P[6][:, 0:1]),
                          inc="e_v", waits=[tb2])
            tac = B.step("vector", lambda e: e.tensor_copy(COLS[:, 3:4], P[6][:, 1:2]),
                         inc="e_v", waits=[])
            tv = B.step("vector", lambda e: e.tensor_scalar(
                COLS[:, 0:1], ARNG[:, :], COLS[:, 4:5], 0.0, op0=ALU.mult,
                op1=ALU.add), inc="e_v", waits=[tlgc])
            tdec = B.step("scalar", lambda e: e.activation(
                COLS[:, 0:1], COLS[:, 0:1], AF.Exp), inc="e_s", waits=[tv])
            tv = B.step("vector", lambda e: e.tensor_scalar(
                COLS[:, 1:2], REVC[:, :], COLS[:, 4:5], 0.0, op0=ALU.mult,
                op1=ALU.add), inc="e_v", waits=[tlgc])
            tgw = B.step("scalar", lambda e: e.activation(
                COLS[:, 1:2], COLS[:, 1:2], AF.Exp), inc="e_s", waits=[tv])
            tv = B.step("vector", lambda e: e.tensor_scalar(
                COLS[:, 2:3], COLS[:, 4:5], float(CC), 0.0, op0=ALU.mult,
                op1=ALU.add), inc="e_v", waits=[tlgc])
            tgc = B.step("scalar", lambda e: e.activation(
                COLS[:, 2:3], COLS[:, 2:3], AF.Exp), inc="e_s", waits=[tv])
            tv = B.step("vector", lambda e: e.tensor_scalar(
                MSK[:, :], MSK[:, :], COLS[:, 4:5], 0.0, op0=ALU.mult, op1=ALU.add),
                inc="e_v", waits=[tlgc, tok_pro])
            tmsk = B.step("scalar", lambda e: e.activation(
                MSK[:, :], MSK[:, :], AF.Exp), inc="e_s", waits=[tv])
            tok_ww = B.dma("gpsimd", WWB[:, :, :], r3(d["wwT"]), "d_ww")
            tz1 = B.step("vector", lambda e: e.memset(
                AR1[:, :, :].rearrange("p k o -> p (k o)"), 0.0), inc="e_v",
                waits=[bar_s])
            tz2 = B.step("vector", lambda e: e.memset(
                WSTB[:, :, :].rearrange("p k o -> p (k o)"), 0.0), inc="e_v",
                waits=[B.bar("e_t")])
            return _l1a_body(tlg, tal, tlgc, tac, tdec, tgw, tgc, tmsk,
                             tok_ww, tz1, tz2)

        def _l1a_body(tlg, tal, tlgc, tac, tdec, tgw, tgc, tmsk,
                      tok_ww, tz1, tz2):
            tok_wstb_prev = tz2
            tok_x2c_free = [B.bar("e_t"), B.bar("e_t")]
            tok_p01_free = B.bar("e_v")
            tok_p23_free = B.bar("e_v")
            tok_upd_free = [None, None]
            tok_tr_free = [None, None, None, None]
            last_rd_copy = None

            def rk(k, b):
                return x2c[:, b, k, 1:129]

            def wk(k, b):
                return x2c[:, b, k, 0:128]

            p6b = P[6][:, :].bitcast(BF16)
            p7b = P[7][:, :].bitcast(BF16)

            for c in range(NCH):
                b = c % 2
                if "cast" not in L1A_PH:
                    tcast = None
                elif c == 0:
                    tzc = B.step("vector", lambda e, b=b: e.memset(
                        x2c[:, b, :, 0:1].rearrange("p k c -> p (k c)"), 0.0),
                        inc="e_v", waits=tok_xs + [tok_x2c_free[b]])
                    cast_k = []
                    for k in range(HC):
                        tcast = B.step("vector", lambda e, k=k, b=b: e.tensor_copy(
                            x2c[:, b, k, 1:129], AR0[:, k, 0:128]),
                            inc="e_v", waits=([tzc] if k == 0 else []))
                        cast_k.append(tcast)
                else:
                    cast_k = []
                    for k in range(HC):
                        tcast = B.step("vector", lambda e, k=k, b=b, c=c:
                                       e.tensor_copy(
                                           x2c[:, b, k, 0:129],
                                           AR0[:, k, c * CC - 1:c * CC + 128]),
                                       inc="e_v",
                                       waits=(tok_xs + [tok_x2c_free[b]]
                                              if k == 0 else []))
                        cast_k.append(tcast)
                tmm = None
                if "v" in L1A_PH:
                 for dh in range(2):
                    for k in range(HC):
                        if dh == 0:
                            w0 = [cast_k[k]] + ([tok_ww, tok_p01_free]
                                                if k == 0 else [])
                        else:
                            w0 = []
                        tmm = B.step("tensor", lambda e, dh=dh, k=k, b=b: e.matmul(
                            P[dh][:, :], rk(k, b),
                            WWB[:, k, dh * 512:(dh + 1) * 512],
                            start=(k == 0), stop=(k == HC - 1),
                            skip_group_check=True),
                            inc=("e_t" if (dh == 1 and k == HC - 1) else None),
                            waits=w0)
                tut = None
                tvt = None
                tok_p01_free_v = None
                if "evv" in L1A_PH:
                 for dh in range(2):
                    tut = B.step("vector", lambda e, dh=dh: e.tensor_scalar(
                        utm[:, dh, :], P[dh][:, :], COLS[:, 1:2], 0.0,
                        op0=ALU.mult, op1=ALU.add), inc="e_v",
                        waits=([tmm, tgw] if dh == 0 else []))
                 for dh in range(2):
                    tvt = B.step("vector", lambda e, dh=dh: e.tensor_copy(
                        vtm[:, dh, :], P[dh][:, :]), inc="e_v", waits=[])
                 tok_p01_free_v = tvt
                tsm = None
                if "S" in L1A_PH:
                 for k in range(HC):
                    tsm = B.step("tensor", lambda e, k=k, b=b: e.matmul(
                        P[6][:, 0:128], rk(k, b), wk(k, b),
                        start=(k == 0), stop=(k == HC - 1), skip_group_check=True),
                        inc=("e_t" if k == HC - 1 else None),
                        waits=([cast_k[0], B.p67_free, tok_tr_free[0],
                                tok_tr_free[2], tok_tr_free[1], tok_tr_free[3]]
                               if k == 0 else []))
                tsmtb = None
                if "smt" in L1A_PH:
                 tsmb = B.step("vector", lambda e: e.tensor_tensor(
                    SMB[:, :], P[6][:, 0:128], MSK[:, :], op=ALU.mult),
                    inc="e_v", waits=[tsm, tmsk])
                 tsmt = B.step("tensor", lambda e: e.transpose(
                    p6b[:, 256:384], SMB[:, :], IDB[:, :]), inc="e_t",
                    waits=[tsmb, tok_idb])
                 tsmtb = B.step("vector", lambda e: e.tensor_copy(
                    SMTB[:, :], p6b[:, 256:384]), inc="e_v", waits=[tsmt])
                 B.p67_free = tsmtb
                tint = None
                if "inter" in L1A_PH:
                 for dh in range(2):
                    for k in range(HC):
                        tint = B.step("tensor", lambda e, dh=dh, k=k, b=b: e.matmul(
                            P[2 + dh][:, :], rk(k, b),
                            WSTB[:, k, dh * 512:(dh + 1) * 512],
                            start=(k == 0), stop=(k == HC - 1),
                            skip_group_check=True),
                            inc=("e_t" if (dh == 1 and k == HC - 1) else None),
                            waits=([cast_k[0], tok_wstb_prev, tok_p23_free]
                                   if (dh == 0 and k == 0) else []))
                tdecs = None
                if "dec" in L1A_PH:
                 for dh in range(2):
                    tdecs = B.step("vector", lambda e, dh=dh: e.tensor_scalar(
                        P[2 + dh][:, :], P[2 + dh][:, :], COLS[:, 0:1], 0.0,
                        op0=ALU.mult, op1=ALU.add), inc="e_v",
                        waits=([tint, tdec] if dh == 0 else []))
                tintra = None
                if "intra" in L1A_PH:
                 for dh in range(2):
                    tintra = B.step("tensor", lambda e, dh=dh: e.matmul(
                        P[2 + dh][:, :], SMTB[:, :], vtm[:, dh, :],
                        start=False, stop=True, skip_group_check=True),
                        inc=("e_t" if dh == 1 else None),
                        waits=([tdecs, tsmtb, tvt] if dh == 0 else []))
                trt = None
                if "rdev" in L1A_PH:
                 for dh in range(2):
                    trt = B.step("vector", lambda e, dh=dh: e.tensor_copy(
                        rtm.rearrange("p (a c) -> p a c", a=2)[:, dh, :],
                        P[2 + dh][:, :]), inc="e_v",
                        waits=([tintra] if dh == 0 else []))
                 tok_p23_free = trt
                def tr_slot(i):
                    bank = P[6 + (i % 2)][:, :].bitcast(BF16)
                    sub = (i // 2) % 2
                    return bank[:, sub * 128:sub * 128 + 128], i % 4

                if "rdtr" in L1A_PH:
                 for i in range(RDTR_N):
                    pslot, si = tr_slot(i)
                    ttr = B.step("tensor", lambda e, i=i, pslot=pslot: e.transpose(
                        pslot, rtm[:, i * 128:(i + 1) * 128], IDB[:, :]),
                        inc="e_t",
                        waits=[trt, tok_tr_free[si], tok_idb, B.p67_free])
                    if RDTR_COPY:
                        dst = reads[:, i, c * CC:(c + 1) * CC]
                        tcp = B.step("vector", lambda e, dst=dst, pslot=pslot:
                                     e.tensor_copy(dst, pslot),
                                     inc="e_v", waits=[ttr])
                        tok_tr_free[si] = tcp
                        last_rd_copy = tcp
                twk = None
                if "wktr" in L1A_PH:
                 for i in range(HC):
                    pslot, si = tr_slot(i)
                    ttr = B.step("tensor", lambda e, i=i, b=b, pslot=pslot:
                                 e.transpose(pslot, wk(i, b), IDB[:, :]),
                                 inc="e_t", waits=[tok_tr_free[si], tok_idb])
                    twk = B.step("vector", lambda e, i=i, pslot=pslot: e.tensor_copy(
                        wkt[:, i * 128:(i + 1) * 128], pslot),
                        inc="e_v", waits=[ttr])
                    tok_tr_free[si] = twk
                tok_wstb = None
                if "upd" in L1A_PH:
                 for m in range(HC):
                    par = m % 2
                    banks = (P[4], P[5]) if par == 0 else (P[0], P[1])
                    if par == 0:
                        wpar = [tok_upd_free[0], B.p4_free, B.p5_free]
                    else:
                        wpar = [tok_upd_free[1], tok_p01_free_v, tut]
                    tup = None
                    for dh in range(2):
                        tup = B.step("tensor", lambda e, m=m, dh=dh, banks=banks:
                                     e.matmul(banks[dh][:, :],
                                              wkt[:, m * 128:(m + 1) * 128],
                                              utm[:, dh, :],
                                              start=True, stop=True,
                                              skip_group_check=True),
                                     inc=("e_t" if dh == 1 else None),
                                     waits=([twk, tut] + wpar if dh == 0 else []))
                    tsv = None
                    for dh in range(2):
                        tsv = B.step("vector", lambda e, m=m, dh=dh, banks=banks:
                                     e.scalar_tensor_tensor(
                                         AR1[:, m, dh * 512:(dh + 1) * 512],
                                         AR1[:, m, dh * 512:(dh + 1) * 512],
                                         COLS[:, 2:3], banks[dh][:, :],
                                         op0=ALU.mult, op1=ALU.add),
                                     inc="e_v",
                                     waits=([tup, tgc, tz1] if dh == 0 else []))
                    if par == 0:
                        B.p4_free = tsv
                        B.p5_free = tsv
                    tok_upd_free[par] = tsv
                    tok_wstb = B.step("scalar", lambda e, m=m: e.activation(
                        WSTB[:, m, :], AR1[:, m, :], AF.Identity), inc="e_s",
                        waits=[tsv, tz2])
                if tok_wstb is not None:
                    tok_wstb_prev = tok_wstb
                tok_p01_free = tok_upd_free[1] or tok_p01_free_v or tok_p01_free
                tok_x2c_free[b] = twk or B.bar("e_t")
            if not L1A_FIN:
                return
            tok_wr = B.dma("gpsimd", WWB[:, :, :], r3(d["wrT"]), "d_wr",
                           waits=[B.bar("e_t")])
            bar_t = B.bar("e_t"); bar_v2 = B.bar("e_v"); bar_s2 = B.bar("e_s")
            pend = None
            for tau in range(NT):
                tsl = slice(tau * 512, (tau + 1) * 512)
                papply = pend() if pend is not None else None
                ln_collect_start()
                evac = {}
                for hc in range(HC):
                    if hc == 2 and papply is not None:
                        papply()
                        papply = None
                    bank = P[hc % 2]
                    mmt = None
                    for k in range(HC):
                        mmt = B.step("tensor", lambda e, k=k, hc=hc, bank=bank,
                                     tau=tau:
                                     e.matmul(bank[:, :],
                                              WWB[:, k, hc * 128:(hc + 1) * 128],
                                              reads[:, k, tau * 512:(tau + 1) * 512],
                                              start=(k == 0), stop=(k == HC - 1),
                                              skip_group_check=True),
                                     inc=("e_t" if k == HC - 1 else None),
                                     waits=([tok_wr, last_rd_copy, bar_v2, bar_s2,
                                             evac.get(hc - 2)] if k == 0 else []))
                    t1 = B.step("vector", lambda e, hc=hc, tsl=tsl:
                                e.scalar_tensor_tensor(T1F[:, :], P[hc % 2][:, :],
                                                       COLS[:, 3:4],
                                                       AR0[:, hc, tsl],
                                                       op0=ALU.mult, op1=ALU.add),
                                inc="e_v", waits=[mmt, tac])
                    t2 = B.step("vector", lambda e, hc=hc, tsl=tsl: e.tensor_tensor(
                        AR0[:, hc, tsl], AR0[:, hc, tsl], T1F[:, :], op=ALU.add),
                        inc="e_v", waits=[])
                    evac[hc] = t2
                    collect_std(hc, AR0[:, hc, tsl], t2)

                def pend(tsl=tsl, tau=tau):
                    trv, trs = ln_rows(alt=1)
                    return lambda: ln_apply(AR0[:, :, tsl], trv, trs,
                                            affine=False, alt=1, tau=tau)
            pend()()

        QBUF = [AR1[:, 0:2, :].rearrange("p a b -> p (a b)"),
                AR1[:, 2:4, :].rearrange("p a b -> p (a b)")]

        QSLOT = {0: 11, 1: 13, 2: 12}

        def _qviews(wi, bb):
            if wi in (0, 2):
                return QBUF[bb].rearrange("p (kc f) -> p kc f", kc=HC)
            return QBUF[bb].rearrange("p (kc o) -> p kc o", kc=2)

        def quant_prep_pass1():
            B.qp = {}
            prev = [None, None]
            for wi, wname in [(0, "wfcT"), (1, "wprojT"), (2, "l0wT")]:
                slot = QSLOT[wi]
                redop = ALU.add if wi == 2 else ALU.max
                tz = B.step("vector", lambda e, slot=slot: e.memset(
                    COLS[:, slot:slot + 1], 0.0), inc="e_v", waits=[])
                if wi == 0:
                    dv = d[wname].rearrange("(kc k) f -> k kc f", k=128)
                    chunks = [dv[:, :, j * 256:(j + 1) * 256] for j in range(16)]
                elif wi == 2:
                    dv = d[wname].rearrange("(kc k) o -> k kc o", k=128)
                    chunks = [dv[:, :, j * 256:(j + 1) * 256] for j in range(4)]
                else:
                    dv = d[wname].rearrange("(kc k) o -> k kc o", k=128)
                    chunks = [dv[:, j * 2:(j + 1) * 2, :] for j in range(16)]
                vtk = [None, None]
                for j, ch in enumerate(chunks):
                    bb = j % 2
                    tld = B.dma("sync", _qviews(wi, bb), ch, f"d_q{wi}{bb}",
                                waits=[vtk[bb], prev[bb]]
                                + ([tok_x] if wi == 0 and j < 2 else []))
                    tr = B.step("vector", lambda e, bb=bb, redop=redop:
                                e.tensor_reduce(
                                    COLS[:, 15:16], QBUF[bb], axis=AX, op=redop,
                                    apply_absolute_value=True), inc="e_v",
                                waits=[tld, tz])
                    vtk[bb] = B.step("vector", lambda e, slot=slot, redop=redop:
                                     e.tensor_tensor(
                                         COLS[:, slot:slot + 1],
                                         COLS[:, slot:slot + 1],
                                         COLS[:, 15:16], op=redop),
                                     inc="e_v", waits=[tr])
                B.qp[wi] = vtk
                prev = vtk

        QS_SR = {0: (7, 8), 1: (9, 10), 2: (5, 6)}

        def quant_prep_pe_and_pass2():
            otk_all = []
            prev2 = list(B.qp[2])
            prev2o = [None, None]
            for wi in (0, 1, 2):
                vtk = B.qp[wi]
                slot_s, slot_r = QS_SR[wi]
                ttr = B.step("tensor", lambda e, wi=wi: e.transpose(
                    P[6][0:1, 0:128], COLS[:, QSLOT[wi]:QSLOT[wi] + 1], IDF[:, :]),
                    inc="e_t", waits=[vtk[0], vtk[1], tok_pro, B.p67_free])
                if wi == 2:
                    tmx = B.step("vector", lambda e: e.tensor_reduce(
                        CELLS[0:1, 8:9], P[6][0:1, 0:128], axis=AX, op=ALU.add),
                        inc="e_v", waits=[ttr])
                    tsc = B.step("vector", lambda e: e.tensor_scalar(
                        CELLS[0:1, 8:9], CELLS[0:1, 8:9], 1.0 / float(H * H),
                        1e-5, op0=ALU.mult, op1=ALU.max), inc="e_v", waits=[tmx])
                else:
                    tmx = B.step("vector", lambda e: e.tensor_reduce(
                        CELLS[0:1, 8:9], P[6][0:1, 0:128], axis=AX, op=ALU.max),
                        inc="e_v", waits=[ttr])
                    tsc = B.step("vector", lambda e: e.tensor_scalar(
                        CELLS[0:1, 8:9], CELLS[0:1, 8:9], 1.0 / 127.0, 0.0,
                        op0=ALU.mult, op1=ALU.add), inc="e_v", waits=[tmx])
                tri = B.step("vector", lambda e: e.reciprocal(
                    CELLS[0:1, 9:10], CELLS[0:1, 8:9]), inc="e_v", waits=[tsc])
                tb1 = B.step("tensor", lambda e: e.matmul(
                    P[6][:, 0:1], ONFR[0:1, :], CELLS[0:1, 8:9], start=True,
                    stop=True, skip_group_check=True), inc=None,
                    waits=[tri, tmx])
                tb2 = B.step("tensor", lambda e: e.matmul(
                    P[6][:, 1:2], ONFR[0:1, :], CELLS[0:1, 9:10], start=False,
                    stop=True, skip_group_check=True), inc="e_t", waits=[])
                tc1 = B.step("vector", lambda e, slot_s=slot_s: e.tensor_copy(
                    COLS[:, slot_s:slot_s + 1], P[6][:, 0:1]), inc="e_v",
                    waits=[tb2])
                tc2 = B.step("vector", lambda e, slot_r=slot_r: e.tensor_copy(
                    COLS[:, slot_r:slot_r + 1], P[6][:, 1:2]), inc="e_v", waits=[])
                B.p67_free = tc2
                B.qtc = getattr(B, "qtc", {})
                B.qtc[wi] = (tc1, tc2)
            for wi, wq, nch, lo, hi in [(2, l0wqb, 4, -1.0, 1.0),
                                        (0, wqfc, 16, -128.0, 127.0),
                                        (1, wqproj, 16, -128.0, 127.0)]:
                slot_s, slot_r = QS_SR[wi]
                tc1, tc2 = B.qtc[wi]
                if wi == 0:
                    ovv = wq.rearrange("(kc k) f -> k kc f", k=128)
                    ov = [ovv[:, :, j * 256:(j + 1) * 256] for j in range(nch)]
                    sv = d["wfcT"].rearrange("(kc k) f -> k kc f", k=128)
                    cv = [sv[:, :, j * 256:(j + 1) * 256] for j in range(nch)]
                elif wi == 2:
                    ovv = wq.rearrange("(kc k) o -> k kc o", k=128)
                    ov = [ovv[:, :, j * 256:(j + 1) * 256] for j in range(nch)]
                    sv = d["l0wT"].rearrange("(kc k) o -> k kc o", k=128)
                    cv = [sv[:, :, j * 256:(j + 1) * 256] for j in range(nch)]
                else:
                    ovv = wq.rearrange("(kc k) o -> k kc o", k=128)
                    ov = [ovv[:, j * 2:(j + 1) * 2, :] for j in range(nch)]
                    sv = d["wprojT"].rearrange("(kc k) o -> k kc o", k=128)
                    cv = [sv[:, j * 2:(j + 1) * 2, :] for j in range(nch)]
                wpv = [wpq[:, 0, :], wpq[:, 1, :]]
                wpv3 = [_qv_out(wi, bb) for bb in range(2)]
                vtk2 = [None, None]
                otk = [None, None]
                for j in range(nch):
                    bb = j % 2
                    buf = QBUF[bb]
                    tld = B.dma("sync", _qviews(wi, bb), cv[j], f"d_q{wi}{bb}",
                                waits=[vtk2[bb], otk[bb], prev2[bb]])
                    vtk2[bb] = emit_round(buf, wpv[bb],
                                          COLS[:, slot_r:slot_r + 1],
                                          COLS[:, slot_s:slot_s + 1],
                                          lo, hi,
                                          [tld, tc1, tc2, otk[bb], prev2o[bb]])
                    otk[bb] = B.dma("sync", ov[j], wpv3[bb], f"d_qo{wi}{bb}",
                                    waits=[vtk2[bb]])
                otk_all += otk
                prev2 = list(vtk2)
                prev2o = list(otk)
            B.qprep_done = tuple(otk_all)

        def _qv_out(wi, bb):
            if wi in (0, 2):
                return wpq[:, bb, :].rearrange("p (kc f) -> p kc f", kc=HC)
            return wpq[:, bb, :].rearrange("p (kc o) -> p kc o", kc=2)

        def stage_l1f():
            tok_apj = B.dma("gpsimd", APJT[:, :, :], r3(d["aprojT"]), "d_apj",
                            waits=[B.bar("e_t"), B.bar("sca")])
            tok_bpjt = B.dma("gpsimd", BPJT[:, :], d["bprojT"][:, :], "d_bpjt")
            wq_fcv = wqfc.rearrange("(kc k) f -> k kc f", k=128)
            wq_pjv = wqproj.rearrange("(kc k) o -> k kc o", k=128)
            B.ws_free = [B.bar("e_t")] * 3
            pend = None
            for tau in range(NT):
                tsl = slice(tau * 512, (tau + 1) * 512)
                txc = cast_tile(XBT, AR0[:, :, tsl],
                                [B.bar("e_t")] + tok_xs,
                                per_hc=B.resid_hc[tau])
                cast_hc = B.cast_hc
                papply = pend() if pend is not None else None
                txa = None
                for k in range(HC):
                    txa = B.step("tensor", lambda e, k=k: e.matmul(
                        P[6][0:16, :], AFCT[:, k, :], XBT[:, k, :],
                        start=(k == 0), stop=(k == HC - 1), skip_group_check=True),
                        inc=("e_t" if k == HC - 1 else None),
                        waits=([cast_hc[0], tok_afc, B.p67_free] if k == 0
                               else [cast_hc[k]]))
                txab = B.step("vector", lambda e: e.tensor_scalar(
                    XAB[:, :], P[6][0:16, :], 2.0, 0.0, op0=ALU.mult, op1=ALU.add),
                    inc="e_v", waits=[txa])
                B.p67_free = txab
                gelu_toks = {}
                tha = None
                wtoks = {}
                for m in range(2):
                    wtoks[m] = B.dma("sync", WS[m % 3][:, :, :],
                                     wq_fcv[:, :, m * 128:(m + 1) * 128],
                                     "d_lf" + str(m % 3),
                                     waits=[B.ws_free[m % 3]] + list(B.qprep_done))
                for m in range(FC):
                    par = m % 3
                    if m + 2 < FC:
                        wtoks[m + 2] = B.dma(
                            "sync", WS[(m + 2) % 3][:, :, :],
                            wq_fcv[:, :, (m + 2) * 128:(m + 3) * 128],
                            "d_lf" + str((m + 2) % 3),
                            waits=[B.ws_free[(m + 2) % 3]])
                    bank = P[m % 2]
                    tmm = None
                    for k in range(HC):
                        tmm = B.step("tensor", lambda e, k=k, par=par, bank=bank:
                                     e.matmul(bank[:, :], WS[par][:, k, :],
                                              XBT[:, k, :],
                                              start=(k == 0), stop=False,
                                              skip_group_check=True),
                                     inc=None,
                                     waits=([wtoks[m], gelu_toks.get(m - 2)]
                                            if k == 0 else []))
                    tmm = B.step("tensor", lambda e, m=m, bank=bank: e.matmul(
                        bank[:, :], BFCT[:, m * 128:(m + 1) * 128], XAB[:, :],
                        start=False, stop=True, skip_group_check=True),
                        inc="e_t", waits=[txab, tok_bfct])
                    B.ws_free[par] = tmm
                    tg = B.step("scalar", lambda e, m=m, bank=bank: e.activation(
                        hm[:, m, :], bank[:, :], AF.Gelu, bias=BFCc[:, m:m + 1]),
                        inc="e_s", waits=[tmm, tok_pro])
                    if m > 0:
                        tha = B.step("tensor", lambda e, m=m: e.matmul(
                            P[6][0:16, :], APJT[:, m - 1, :], hm[:, m - 1, :],
                            start=(m == 1), stop=False, skip_group_check=True),
                            inc=None, waits=[gelu_toks[m - 1], tok_apj])
                    gelu_toks[m] = tg
                tha = B.step("tensor", lambda e: e.matmul(
                    P[6][0:16, :], APJT[:, FC - 1, :], hm[:, FC - 1, :],
                    start=False, stop=True, skip_group_check=True),
                    inc="e_t", waits=[gelu_toks[FC - 1]])
                thab = B.step("vector", lambda e: e.tensor_scalar(
                    HAB[:, :], P[6][0:16, :], 2.0, 0.0, op0=ALU.mult, op1=ALU.add),
                    inc="e_v", waits=[tha])
                B.p67_free = thab
                if papply is not None:
                    papply()
                    papply = None
                mm2_last = None
                wtoks = {}
                for k in range(2):
                    wtoks[k] = B.dma("sync", WS[k % 3][:, :, :],
                                     wq_pjv[:, k, :].rearrange(
                                         "p (a b) -> p a b", a=HC),
                                     "d_lg" + str(k % 3),
                                     waits=[B.ws_free[k % 3]])
                for k in range(FC):
                    par = k % 3
                    if k + 2 < FC:
                        wtoks[k + 2] = B.dma(
                            "sync", WS[(k + 2) % 3][:, :, :],
                            wq_pjv[:, k + 2, :].rearrange(
                                "p (a b) -> p a b", a=HC),
                            "d_lg" + str((k + 2) % 3),
                            waits=[B.ws_free[(k + 2) % 3]])
                    for hc in range(HC):
                        w0 = []
                        if k == 0 and hc == 0:
                            w0 = [wtoks[k], gelu_toks[FC - 1], gelu_toks[FC - 2],
                                  thab, B.bar("e_v"), B.bar("e_s")]
                        elif hc == 0:
                            w0 = [wtoks[k]]
                        mm2_last = B.step("tensor",
                                          lambda e, k=k, hc=hc, par=par: e.matmul(
                                              P[hc][:, :], WS[par][:, hc, :],
                                              hm[:, k, :],
                                              start=(k == 0), stop=False,
                                              skip_group_check=True),
                                          inc=("e_t" if hc == HC - 1 else None),
                                          waits=w0)
                    B.ws_free[par] = mm2_last
                for hc in range(HC):
                    mm2_last = B.step("tensor", lambda e, hc=hc: e.matmul(
                        P[hc][:, :], BPJT[:, hc * 128:(hc + 1) * 128], HAB[:, :],
                        start=False, stop=True, skip_group_check=True),
                        inc=("e_t" if hc == HC - 1 else None),
                        waits=([tok_bpjt] if hc == 0 else []))
                ln_collect_start()
                ev = {}

                def evac_l1f(hc, tsl=tsl):
                    return B.step("vector", lambda e, hc=hc, tsl=tsl:
                                  e.scalar_tensor_tensor(
                                      AR0[:, hc, tsl], P[hc][:, :],
                                      BPJ[:, hc:hc + 1], AR0[:, hc, tsl],
                                      op0=ALU.add, op1=ALU.add),
                                  inc="e_v", waits=[mm2_last])
                for hc in (6, 7):
                    ev[hc] = evac_l1f(hc)
                B.p67_free = ev[7]
                for hc in (6, 7, 0, 1, 2, 3, 4, 5):
                    if hc not in ev:
                        ev[hc] = evac_l1f(hc)
                    collect_std(hc, AR0[:, hc, tsl], ev[hc])

                def pend(tsl=tsl, tau=tau):
                    trv, trs = ln_rows(alt=1)
                    return lambda: ln_apply(AR0[:, :, tsl], trv, trs,
                                            affine=False, alt=1, tau=tau)
            pend()()

        def epilogue():
            yv = y.rearrange("(kc k) t -> k kc t", k=128)
            tok_dma = None
            for tau in range(NT):
                tsl = slice(tau * 512, (tau + 1) * 512)
                w = list(B.resid_hc[tau]) if B.resid_hc.get(tau) \
                    else [B.bar("e_v")]
                tok_dma = B.dma("gpsimd", yv[:, :, tsl], AR0[:, :, tsl],
                                "d_out", waits=w)
            B.step("gpsimd", lambda e: e.engine_nop(), inc="e_g",
                   waits=[tok_dma])

        fused_prep = "l0a" in stages and "l1f" in stages
        B.prepass_done = (None, None)
        if "l0a" in stages:
            weight_prepass()
            if fused_prep:
                quant_prep_pass1()
                stage_l0a(mid_hook=quant_prep_pe_and_pass2)
            else:
                stage_l0a()
        if "l0f" in stages:
            stage_l0f()
        if "l1a" in stages:
            stage_l1a()
        if "l1f" in stages:
            if not fused_prep:
                quant_prep_pass1()
                quant_prep_pe_and_pass2()
            stage_l1f()
        epilogue()

        with nc.Block() as block:
            B.replay(block)
    return nc



def _prep_inputs(inputs):
    f = np.float32
    ac = np.ascontiguousarray
    bb = ml_dtypes.bfloat16
    pr = {}
    pr["bmT"] = ac(np.asarray(inputs["L0_attn_Bm"], f).T)
    pr["aT"] = ac(np.asarray(inputs["L0_attn_A"], f).T)
    pr["cT"] = ac(np.asarray(inputs["L0_attn_C"], f).T)
    pr["dT"] = ac(np.asarray(inputs["L0_attn_D"], f).T)
    pr["lnw_c"] = ac(np.asarray(inputs["L0_attn_ln_w"], f).reshape(HC, 128).T)
    pr["lnb_c"] = ac(np.asarray(inputs["L0_attn_ln_b"], f).reshape(HC, 128).T)
    pr["w1T"] = ac(np.asarray(inputs["L0_attn_W1"], f).T)
    pr["b1_c"] = ac(np.asarray(inputs["L0_attn_b1"], f).reshape(FC, 128).T)
    pr["w2T"] = ac(np.asarray(inputs["L0_attn_W2"], f).T)
    pr["b2_c"] = ac(np.asarray(inputs["L0_attn_b2"], f).reshape(HC, 128).T)
    pr["l0wT"] = ac(np.asarray(inputs["L0_ffn_weight"], f).T)
    pr["wwT"] = ac(np.asarray(inputs["L1_attn_W_write"], f).T)
    pr["wrT"] = ac(np.asarray(inputs["L1_attn_W_read"], f).T)
    pr["decay"] = np.asarray(inputs["L1_attn_decay"], f).reshape(1, 1)
    pr["logalpha"] = np.asarray(inputs["L1_attn_log_alpha"], f).reshape(1, 1)
    pr["wfcT"] = ac(np.asarray(inputs["L1_ffn_W_fc"], f).T)
    pr["bfc_c"] = ac(np.asarray(inputs["L1_ffn_b_fc"], f).reshape(FC, 128).T)
    pr["afcT"] = ac(np.asarray(inputs["L1_ffn_A_fc"], f).T)
    pr["bfcT"] = ac(np.asarray(inputs["L1_ffn_B_fc"], f).T)
    pr["wprojT"] = ac(np.asarray(inputs["L1_ffn_W_proj"], f).T)
    pr["bprj_c"] = ac(np.asarray(inputs["L1_ffn_b_proj"], f).reshape(HC, 128).T)
    pr["aprojT"] = ac(np.asarray(inputs["L1_ffn_A_proj"], f).T)
    pr["bprojT"] = ac(np.asarray(inputs["L1_ffn_B_proj"], f).T)
    pr["identb"] = np.eye(128, dtype=f).astype(bb)
    pr["identf"] = np.eye(128, dtype=f)
    pr["onesb_col"] = np.ones((128, 1), f).astype(bb)
    pr["onesf_row"] = np.ones((1, 128), f)
    pr["arange"] = np.arange(128, dtype=f).reshape(128, 1)
    pr["rev"] = (CC - 1.0 - np.arange(128, dtype=f)).reshape(128, 1)
    p = np.arange(CC)
    diffs = np.maximum(p[:, None] - 1 - p[None, :], 0).astype(f)
    diffs[p[:, None] <= p[None, :]] = 1e9
    pr["diffs"] = ac(diffs)
    return pr


def kernel(**inputs):
    x = np.asarray(inputs["L0_attn_x"], np.float32)
    pr = _prep_inputs(inputs)
    ncores = 8
    xTs = [np.ascontiguousarray(x[c].T) for c in range(ncores)]

    stream = None
    out = None
    for gi, stages in enumerate(GROUPS):
        first = gi == 0
        last = gi == len(GROUPS) - 1
        nc = build_group(stages, first, last)
        in_maps = []
        for c in range(ncores):
            m = dict(pr)
            m["xT"] = xTs[c]
            if not first:
                m["sin"] = stream[c]
            in_maps.append(m)
        res = run_bass_kernel_spmd(nc, in_maps, core_ids=list(range(ncores)))
        if last:
            out = np.stack([res.results[c]["y"].T for c in range(ncores)],
                           axis=0)
        else:
            stream = [np.ascontiguousarray(res.results[c]["sout"])
                      for c in range(ncores)]
    return out


# version 25
# speedup vs baseline: 1.0425x; 1.0425x over previous
import numpy as np
import ml_dtypes
import concourse.bass as bass
import concourse.mybir as mybir
from concourse.bass_utils import run_bass_kernel_spmd

F32 = mybir.dt.float32
BF16 = mybir.dt.bfloat16
I32 = mybir.dt.int32
AF = mybir.ActivationFunctionType
ALU = mybir.AluOpType
AX = mybir.AxisListType.X

H = 1024
HC = 8
T = 2048
NT = 4
S = 256
SEG = 256
NSEG = T // SEG
CC = 128
NCH = T // CC
FF = 4096
FC = 32
LN_EPS = 1e-5

GROUPS = [["l0a", "l0f", "l1a", "l1f"]]
L1A_PH = {"v", "evv", "S", "smt", "inter", "dec", "intra", "rdev", "rdtr",
          "wktr", "upd", "gam", "mem", "ww", "cast"}
RDTR_N = HC
RDTR_COPY = True
RDTR_MODE = "norm"
L1A_FIN = True
PRO_MIN = False

ENGS = ("sync", "tensor", "vector", "scalar", "gpsimd")


class Build:
    def __init__(self, nc):
        self.nc = nc
        self.q = {e: [] for e in ENGS}
        self.sems = {}
        self.cnt = {}
        self.waited = {e: {} for e in ENGS}

    def sem(self, name):
        if name not in self.sems:
            self.sems[name] = self.nc.semaphore(name).__enter__()
            self.cnt[name] = 0
        return self.sems[name]

    def bar(self, name):
        self.sem(name)
        return (name, self.cnt[name])

    def _wait(self, eng, tok):
        if tok is None:
            return
        s, v = tok
        if v <= 0:
            return
        if self.waited[eng].get(s, 0) >= v:
            return
        self.waited[eng][s] = v
        h = self.sem(s)
        self.q[eng].append(lambda e, h=h, v=v: e.wait_ge(h, v))

    SELF_CHAIN = {"vector": "e_v", "scalar": "e_s", "gpsimd": "e_g"}

    def step(self, eng, fn, inc=None, waits=(), chain=True):
        chain = self.SELF_CHAIN.get(eng) if chain else None
        waits = list(waits)
        if chain is not None:
            prev = getattr(self, "_last_" + eng, None)
            if prev is not None:
                waits.append(prev)
            if inc is None:
                inc = chain
        for tk in waits:
            self._wait(eng, tk)
        tok = None
        if inc is not None:
            h = self.sem(inc)
            self.cnt[inc] += 1
            tok = (inc, self.cnt[inc])
            self.q[eng].append(lambda e, fn=fn, h=h: fn(e).then_inc(h, 1))
        else:
            self.q[eng].append(lambda e, fn=fn: fn(e))
        if chain is not None:
            setattr(self, "_last_" + eng, tok)
        return tok

    def dma(self, qeng, out, in_, key, waits=()):
        for tk in waits:
            self._wait(qeng, tk)
        h = self.sem(key)
        self.cnt[key] += 16
        tok = (key, self.cnt[key])
        self.q[qeng].append(
            lambda e, o=out, i=in_, h=h: e.dma_start(out=o, in_=i).then_inc(h, 16))
        return tok

    def replay(self, block):
        for eng in ENGS:
            sec = getattr(block, eng)
            queue = self.q[eng]

            def emit(e, queue=queue):
                for fn in queue:
                    fn(e)
            sec(emit)


def build_group(stages, first, last):
    nc = bass.Bass("TRN2", target_bir_lowering=False)
    d = {}

    def din(name, shape, dt=F32):
        d[name] = nc.dram_tensor(name, shape, dt, kind="ExternalInput").ap()
        return d[name]

    din("xT", [H, T])
    din("bmT", [H, S]); din("aT", [S, S]); din("cT", [S, H]); din("dT", [H, H])
    din("lnw_c", [128, HC]); din("lnb_c", [128, HC])
    din("w1T", [H, FF]); din("b1_c", [128, FC]); din("w2T", [FF, H]); din("b2_c", [128, HC])
    din("l0wT", [H, H])
    din("wwT", [H, H]); din("wrT", [H, H]); din("decay", [1, 1]); din("logalpha", [1, 1])
    din("wfcT", [H, FF]); din("bfc_c", [128, FC])
    din("afcT", [H, 16]); din("bfcT", [16, FF])
    din("wprojT", [FF, H]); din("bprj_c", [128, HC])
    din("aprojT", [FF, 16]); din("bprojT", [16, H])
    din("identb", [128, 128], BF16); din("identf", [128, 128])
    din("onesb_col", [128, 1], BF16); din("onesf_row", [1, 128])
    din("arange", [128, 1]); din("rev", [128, 1]); din("diffs", [128, 128])
    if not first:
        din("sin", [H, T])
    if last:
        y = nc.dram_tensor("y", [H, T], F32, kind="ExternalOutput").ap()
    else:
        y = nc.dram_tensor("sout", [H, T], F32, kind="ExternalOutput").ap()
    wqfc = nc.dram_tensor("wqfc", [H, FF], BF16, kind="Internal").ap()
    wqproj = nc.dram_tensor("wqproj", [FF, H], BF16, kind="Internal").ap()
    l0wqb = nc.dram_tensor("l0wqb", [H, H], BF16, kind="Internal").ap()
    w1b = nc.dram_tensor("w1b", [H, FF], BF16, kind="Internal").ap()
    w2b = nc.dram_tensor("w2b", [FF, H], BF16, kind="Internal").ap()
    xdb = nc.dram_tensor("xdb", [128, HC * T], BF16, kind="Internal").ap()

    import contextlib
    st = contextlib.ExitStack()

    def sb(name, shape, dt=F32):
        return st.enter_context(nc.sbuf_tensor(name, shape, dt))

    with st:
        AR0 = sb("AR0", [128, HC, T])
        AR1 = sb("AR1", [128, HC, 1024])
        ARB = sb("ARB", [128, 24576], BF16)
        XBT = sb("XBT", [128, HC, 512], BF16)
        STT = sb("STT", [128, T + 1, 2], BF16)
        ATB = sb("ATB", [128, 2, S], BF16)
        CTB = sb("CTB", [128, 2, H], BF16)
        WWB = sb("WWB", [128, HC, H], BF16)
        WSTB = sb("WSTB", [128, HC, H], BF16)
        COLS = sb("COLS", [128, 16])
        CELLS = sb("CELLS", [1, 16])
        LNW = sb("LNW", [128, HC]); LNB = sb("LNB", [128, HC])
        B2C = sb("B2C", [128, HC]); BPJ = sb("BPJ", [128, HC])
        B1C = sb("B1C", [128, FC]); BFCc = sb("BFCc", [128, FC])
        MSK = sb("MSK", [128, 128])
        IDB = sb("IDB", [128, 128], BF16); IDF = sb("IDF", [128, 128])
        ONBC = sb("ONBC", [128, 1], BF16); ONFR = sb("ONFR", [1, 128])
        ONFC = sb("ONFC", [128, 1])
        ARNG = sb("ARNG", [128, 1]); REVC = sb("REVC", [128, 1])
        AFCT = sb("AFCT", [128, HC, 16], BF16)
        BFCT = sb("BFCT", [16, FF], BF16)

        P = [st.enter_context(nc.psum_tensor(f"PB{i}", [128, 512], F32))
             for i in range(8)]

        hm = ARB[:, 0:16384].rearrange("p (m t) -> p m t", m=FC)
        reads = ARB[:, 0:16384].rearrange("p (k t) -> p k t", k=HC)
        l0wq = ARB[:, 0:8192].rearrange("p (k o) -> p k o", k=HC)
        WS = [ARB[:, 16384:17408].rearrange("p (k c) -> p k c", k=HC),
              ARB[:, 17408:18432].rearrange("p (k c) -> p k c", k=HC)]
        SC = ARB[:, 18432:24576]
        rbq = SC[:, 0:1024].rearrange("p (b t) -> p b t", b=2)
        sqq = SC[:, 1024:2048].rearrange("p (b t) -> p b t", b=2)
        wpq = SC[:, 0:4096].rearrange("p (b c) -> p b c", b=2)
        vtm = SC[:, 2048:4096].rearrange("p (a c) -> p a c", a=2)
        utm = SC[:, 4096:6144].rearrange("p (a c) -> p a c", a=2)
        wkt = SC[:, 2048:4096]
        vtm = SC[:, 2048:3072].rearrange("p (a c) -> p a c", a=2)
        utm = SC[:, 3072:4096].rearrange("p (a c) -> p a c", a=2)
        wkt = SC[:, 4096:5120]
        rtm = SC[:, 5120:6144]
        x2c = XBT[:, :, 0:260].rearrange("p k (b c) -> p b k c", b=2)
        OUTB = SC[:, 2048:4096].bitcast(F32)
        ROWS = SC[0:1, 2048:6144].bitcast(F32)
        BCS = SC[:, 4096:5120].bitcast(F32)
        T1F = SC[:, 5120:6144].bitcast(F32)
        I32S = SC[:, 5120:6144].bitcast(I32)
        BMT = WSTB[:, 1:3, :].rearrange(
            "p a b -> p (a b)").rearrange("p (k s) -> p k s", k=HC)
        XDS = [ARB[:, 0:256], ARB[:, 256:512]]
        DTR = ARB[:, 512:8704].rearrange("p (k n) -> p k n", k=HC)
        WSF = [ARB[:, 16384:16896], ARB[:, 17408:17920]]
        STTF = STT[:, :, :].rearrange("p a b -> p (a b)")
        BPJT = STTF[0:16, 0:1024]
        XAB = STTF[0:16, 1024:1536]
        HAB = STTF[0:16, 1536:2048]
        APJT = STTF[:, 2048:2560].rearrange("p (m r) -> p m r", m=FC)
        SMB = STTF[:, 2560:2688]
        SMTB = STTF[:, 2688:2816]
        WS.append(WSTB[:, 0, :].rearrange("p (k c) -> p k c", k=HC))

        B = Build(nc)
        RSET = [(ROWS[:, 0:512], ROWS[:, 512:1024], ROWS[:, 1024:1536],
                 ROWS[:, 1536:2048])] * 2

        def r3(ap, k=128):
            return ap.rearrange("(kc k) n -> k kc n", k=k)

        QFLG = SC[:, 4096:6144]
        I32Q = SC[:, 4096:5120].bitcast(I32)

        def emit_round(tview, qout, rinv_col, s_col, lo, hi, waits):
            B.step("vector", lambda e: e.tensor_scalar(
                tview, tview, rinv_col, 0.0, op0=ALU.mult, op1=ALU.add),
                inc=None, waits=list(waits))
            for off in range(0, 2048, 512):
                B.step("vector", lambda e, off=off: e.tensor_copy(
                    I32Q[:, :], tview[:, off:off + 512]), inc=None, waits=[])
                B.step("vector", lambda e, off=off: e.tensor_copy(
                    qout[:, off:off + 512], I32Q[:, :]), inc=None, waits=[])
            B.step("vector", lambda e: e.tensor_tensor(
                tview, tview, qout, op=ALU.subtract), inc=None, waits=[])
            B.step("vector", lambda e: e.tensor_scalar(
                QFLG, tview, 0.5, 0.0, op0=ALU.is_ge, op1=ALU.add),
                inc=None, waits=[])
            B.step("vector", lambda e: e.tensor_tensor(
                qout, qout, QFLG, op=ALU.add), inc=None, waits=[])
            B.step("vector", lambda e: e.tensor_scalar(
                QFLG, tview, -0.5, 0.0, op0=ALU.is_le, op1=ALU.add),
                inc=None, waits=[])
            B.step("vector", lambda e: e.tensor_tensor(
                qout, qout, QFLG, op=ALU.subtract), inc=None, waits=[])
            B.step("vector", lambda e, lo=lo, hi=hi: e.tensor_scalar(
                qout, qout, float(lo), float(hi), op0=ALU.max, op1=ALU.min),
                inc=None, waits=[])
            return B.step("vector", lambda e: e.tensor_scalar(
                qout, qout, s_col, 0.0, op0=ALU.mult, op1=ALU.add),
                inc="e_v", waits=[])

        _xsrc = d["xT"] if first else d["sin"]
        tok_xs01 = [B.dma("sync", AR0[:, hc, 0:512],
                          _xsrc[hc * 128:(hc + 1) * 128, 0:512], f"d_x{hc}")
                    for hc in range(HC)]
        tok_xs = [B.dma("sync", AR0[:, hc, 512:T],
                        _xsrc[hc * 128:(hc + 1) * 128, 512:T], f"d_x{hc}")
                  for hc in range(HC)]
        tok_x = tok_xs[-1]
        _pro = [("identf", IDF[:, :]), ("onesf_row", ONFR[:, :]),
                ("lnw_c", LNW[:, :]), ("lnb_c", LNB[:, :]),
                ("b2_c", B2C[:, :]), ("b1_c", B1C[:, :]),
                ("bprj_c", BPJ[:, :]), ("bfc_c", BFCc[:, :]),
                ("arange", ARNG[:, :]), ("rev", REVC[:, :]),
                ("diffs", MSK[:, :])]
        if PRO_MIN:
            _pro = _pro[:1]
        for nm, dst in _pro:
            B.dma("sync", dst, d[nm][:, :], "d_pro")
        B.dma("sync", CELLS[0:1, 0:1], d["decay"][:, :], "d_pro")
        tok_pro = B.dma("sync", CELLS[0:1, 1:2], d["logalpha"][:, :], "d_pro")
        tok_idb = B.dma("gpsimd", IDB[:, :], d["identb"][:, :], "d_idb")
        if PRO_MIN:
            tok_onb = tok_bmt = tok_atb = tok_ctb = tok_afc = tok_bfct = None
        else:
            tok_onb = B.dma("gpsimd", ONBC[:, :], d["onesb_col"][:, :], "d_onb")
            tok_bmt = (B.dma("gpsimd", BMT[:, :, :], r3(d["bmT"]), "d_bmt")
                       if "l0a" in stages else None)
            tok_atb = B.dma("gpsimd", ATB[:, :, :], r3(d["aT"]), "d_atb")
            tok_ctb = B.dma("gpsimd", CTB[:, :, :], r3(d["cT"]), "d_ctb")
            tok_afc = B.dma("gpsimd", AFCT[:, :, :], r3(d["afcT"]), "d_afc")
            tok_bfct = B.dma("gpsimd", BFCT[:, :], d["bfcT"][:, :], "d_bfct")
        tok_ones = B.step("vector", lambda e: e.memset(ONFC[:, :], 1.0), inc="e_v")
        tok_sttz = B.step("vector", lambda e: e.memset(
            STT[:, 0:1, :].rearrange("p a b -> p (a b)"), 0.0), inc="e_v")

        B.p4_free = None
        B.p5_free = None
        B.p67_free = None
        B.last_stats = None
        B.ws_free = [None, None, None]
        B.qprep_done = (None, None)
        B.resid_hc = {}

        def bcast_row(row_ap, bank, waits):
            return B.step("tensor",
                          lambda e, r=row_ap, b=bank: e.matmul(
                              b[:, :], ONFR[0:1, :], r, start=True, stop=True,
                              skip_group_check=True),
                          inc="e_t", waits=list(waits) + [tok_pro])

        def ln_collect_start():
            B.stats_k = 0
            B.stats_last = {0: None, 1: None}
            B.stats_pending = []

        def ln_collect(rb_ap, sq_ap, waits_rb, waits_sq):
            k = B.stats_k
            w1 = list(waits_rb) + [tok_onb]
            if k == 0:
                w1.append(B.p67_free)
            B.step("tensor", lambda e, r=rb_ap: e.matmul(
                P[6][0:1, :], ONBC[:, :], r, start=(k == 0), stop=(k == HC - 1),
                skip_group_check=True), inc=None, waits=w1)
            t2 = B.step("tensor", lambda e, q2=sq_ap: e.matmul(
                P[7][0:1, :], ONBC[:, :], q2, start=(k == 0), stop=(k == HC - 1),
                skip_group_check=True), inc="e_t", waits=list(waits_sq))
            B.stats_k += 1
            B.last_stats = t2

        def ln_rows(eps=LN_EPS, alt=0):
            ROW_MU, ROW_RS, ROW_MR, ROW_TMP = RSET[alt]
            while B.stats_pending:
                p2, rb2, sq2 = B.stats_pending.pop(0)
                ln_collect(rbq[:, p2, :], sqq[:, p2, :], [rb2], [sq2])
                B.stats_last[p2] = B.last_stats
            t_mu = B.step("vector", lambda e: e.tensor_scalar(
                ROW_MU[:, :], P[6][0:1, :], 1.0 / H, 0.0, op0=ALU.mult, op1=ALU.add),
                inc="e_v", waits=[B.last_stats])
            t_ms = B.step("vector", lambda e: e.tensor_scalar(
                ROW_TMP[:, :], P[7][0:1, :], 1.0 / H, 0.0, op0=ALU.mult, op1=ALU.add),
                inc="e_v", waits=[])
            B.p67_free = t_ms
            t_rs = B.step("vector", lambda e: e.scalar_tensor_tensor(
                ROW_RS[:, :], ROW_MU[:, :], -1.0, ROW_MU[:, :],
                op0=ALU.mult, op1=ALU.mult), inc="e_v", waits=[t_mu])
            t_tmp = B.step("vector", lambda e: e.tensor_tensor(
                ROW_TMP[:, :], ROW_TMP[:, :], ROW_RS[:, :], op=ALU.add),
                inc="e_v", waits=[t_rs, t_ms])
            t_eps = B.step("vector", lambda e, eps=eps: e.tensor_scalar(
                ROW_TMP[:, :], ROW_TMP[:, :], eps, 0.0, op0=ALU.add, op1=ALU.add),
                inc="e_v", waits=[t_tmp])
            ts = B.step("scalar", lambda e: e.activation(
                ROW_RS[:, :], ROW_TMP[:, :], AF.Ln), inc="e_s", waits=[t_eps])
            ts = B.step("scalar", lambda e: e.activation(
                ROW_RS[:, :], ROW_RS[:, :], AF.Exp, scale=-0.5), inc="e_s", waits=[ts])
            tv2 = B.step("vector", lambda e: e.scalar_tensor_tensor(
                ROW_MR[:, :], ROW_MU[:, :], -1.0, ROW_RS[:, :],
                op0=ALU.mult, op1=ALU.mult), inc="e_v", waits=[ts, t_mu])
            return tv2, ts

        def ln_apply(dst, tok_rows_v, tok_rows_s, affine=False, alt=0,
                     tau=None):
            ROW_MU, ROW_RS, ROW_MR, ROW_TMP = RSET[alt]
            trs = bcast_row(ROW_RS[0:1, :], P[4],
                            [tok_rows_v, tok_rows_s, B.p4_free])
            tmr = bcast_row(ROW_MR[0:1, :], P[5], [tok_rows_v, B.p5_free])
            tb = B.step("vector", lambda e: e.tensor_copy(BCS[:, :], P[5][:, :]),
                        inc="e_v", waits=[tmr])
            tv = None
            hc_toks = []
            for hc in range(HC):
                B.step("vector", lambda e, hc=hc: e.tensor_tensor(
                    dst[:, hc, :], dst[:, hc, :], P[4][:, :], op=ALU.mult),
                    inc=None, waits=([trs] if hc == 0 else []))
                if affine:
                    B.step("vector", lambda e, hc=hc: e.tensor_tensor(
                        dst[:, hc, :], dst[:, hc, :], BCS[:, :], op=ALU.add),
                        inc=None, waits=([tb] if hc == 0 else []))
                    tv = B.step("vector", lambda e, hc=hc: e.tensor_scalar(
                        dst[:, hc, :], dst[:, hc, :], LNW[:, hc:hc + 1],
                        LNB[:, hc:hc + 1], op0=ALU.mult, op1=ALU.add),
                        inc="e_v", waits=[])
                else:
                    tv = B.step("vector", lambda e, hc=hc: e.tensor_tensor(
                        dst[:, hc, :], dst[:, hc, :], BCS[:, :], op=ALU.add),
                        inc="e_v", waits=([tb] if hc == 0 else []))
                hc_toks.append(tv)
            B.apply_hc = hc_toks
            if tau is not None:
                B.resid_hc[tau] = hc_toks
            B.p4_free = tv
            B.p5_free = tv
            return tv

        def cast_tile(dst, src, waits, per_hc=None):
            toks = []
            tk = None
            for hc in range(HC):
                w = list(waits)
                if per_hc is not None:
                    w.append(per_hc[hc])
                if hc % 2 == 0:
                    tk = B.step("vector", lambda e, hc=hc, d2=dst, s2=src:
                                e.tensor_copy(d2[:, hc, :], s2[:, hc, :]),
                                inc="e_v", waits=w)
                else:
                    tk = B.step("scalar", lambda e, hc=hc, d2=dst, s2=src:
                                e.copy(d2[:, hc, :], s2[:, hc, :]),
                                inc="e_s", waits=w)
                toks.append(tk)
            B.cast_hc = toks
            return tk

        def collect_std(hc, src_ap, tok_src, flush=False):
            par = hc % 2
            trb = B.step("vector", lambda e, s2=src_ap, par=par: e.tensor_copy(
                rbq[:, par, :], s2), inc="e_v",
                waits=[tok_src, B.stats_last[par]])
            tsq = B.step("scalar", lambda e, s2=src_ap, par=par: e.activation(
                sqq[:, par, :], s2, AF.Square), inc="e_s",
                waits=[tok_src, B.stats_last[par]])
            B.stats_pending.append((par, trb, tsq))
            if len(B.stats_pending) > 1:
                p2, rb2, sq2 = B.stats_pending.pop(0)
                ln_collect(rbq[:, p2, :], sqq[:, p2, :], [rb2], [sq2])
                B.stats_last[p2] = B.last_stats
            if flush:
                while B.stats_pending:
                    p2, rb2, sq2 = B.stats_pending.pop(0)
                    ln_collect(rbq[:, p2, :], sqq[:, p2, :], [rb2], [sq2])
                    B.stats_last[p2] = B.last_stats

        def weight_prepass():
            for hc in range(HC):
                tok_dtr = B.dma(
                    "gpsimd", DTR[:, :, hc * 128:(hc + 1) * 128],
                    r3(d["dT"][:, hc * 128:(hc + 1) * 128]), "d_dtr")
            B.dtr_toks = [tok_dtr] * HC
            jobs = []
            for m in range(FC):
                jobs.append((r3(d["w1T"][:, m * 128:(m + 1) * 128]),
                             r3(w1b[:, m * 128:(m + 1) * 128])))
            for k in range(FC):
                sl = d["w2T"][k * 128:(k + 1) * 128, :]
                s2 = w2b[k * 128:(k + 1) * 128, :]
                jobs.append((sl.rearrange("k (a b) -> k a b", a=HC),
                             s2.rearrange("k (a b) -> k a b", a=HC)))
            stok = [None, None]
            for j, (srcv, dstv) in enumerate(jobs):
                bb = j % 2
                tld = B.dma("gpsimd", WS[bb][:, :, :], srcv, f"d_pw{bb}",
                            waits=[stok[bb]] + ([tok_x] if j < 2 else []))
                stok[bb] = B.dma("sync", dstv, WS[bb][:, :, :], f"d_ps{bb}",
                                 waits=[tld])
            B.prepass_done = tuple(stok)

        def stage_l0a(mid_hook=None):
            seg_free = [None, None]
            xb_done = [None, None]
            tmemset = tok_sttz
            xd_evac = {}
            xds_free = [None, None]
            B.xd_store_seg = {}
            B.xd_done = None
            B.xd_store = None
            cast_tok = [None] * NSEG

            def emit_cast(s2, hc):
                par2 = s2 % 2
                half2 = slice(par2 * 256, par2 * 256 + 256)
                tok_src = tok_xs01[hc] if s2 < 2 else tok_xs[hc]
                cast_tok[s2] = B.step(
                    "scalar", lambda e, hc=hc, s2=s2, half2=half2:
                    e.copy(XBT[:, hc, half2],
                           AR0[:, hc, s2 * SEG:(s2 + 1) * SEG]),
                    inc="e_s",
                    waits=[tok_src]
                    + ([xb_done[par2]] if hc == 0 else []))
            for s in range(NSEG):
                if s == 4 and mid_hook is not None:
                    mid_hook()
                par = s % 2
                half = slice(par * 256, par * 256 + 256)
                if s == 0:
                    for hc in range(HC):
                        emit_cast(0, hc)
                tcast = cast_tok[s]
                tmm = None
                for m in range(2):
                    for k in range(HC):
                        lastk = (m == 1 and k == HC - 1)
                        tmm = B.step("tensor", lambda e, m=m, k=k, par=par,
                                     half=half:
                                     e.matmul(P[par][:, m * 256:(m + 1) * 256],
                                              BMT[:, k, m * 128:(m + 1) * 128],
                                              XBT[:, k, half],
                                              start=(m == 0 and k == 0),
                                              stop=(k == HC - 1),
                                              skip_group_check=True),
                                     inc=("e_t" if lastk else None),
                                     waits=([tcast, tok_bmt, tok_atb, seg_free[par],
                                             tmemset] if (m == 0 and k == 0) else []))
                jobs = []
                for hc in range(HC):
                    for k in range(HC):
                        jobs.append((hc, k))
                njobs = len(jobs)
                step_of_job = {}
                first_step = 64 if s == 0 else 0
                avail = SEG - first_step
                for ji in range(njobs):
                    step_of_job.setdefault(first_step + (ji * avail) // njobs,
                                           []).append(ji)
                last_xd = None
                for t in range(SEG):
                    g = s * SEG + t
                    w = [("sca", g)] if g > 0 else []
                    for j in range(2):
                        for kc in range(2):
                            lastmm = (j == 1 and kc == 1)
                            B.step("tensor",
                                   lambda e, j=j, kc=kc, t=t, g=g, par=par:
                                   e.matmul(P[par][:, j * 256 + t:
                                                   j * 256 + t + 1],
                                            ATB[:, kc, j * 128:(j + 1) * 128],
                                            STT[:, g, kc:kc + 1],
                                            start=False, stop=(kc == 1),
                                            skip_group_check=True),
                                   inc=("scm" if lastmm else None),
                                   waits=(w if (j == 0 and kc == 0) else []))
                    for ji in step_of_job.get(t, []):
                        hc, k = jobs[ji]
                        bank = P[2 + hc % 4]
                        wx = []
                        if k == 0:
                            wx = [B.dtr_toks[hc], xd_evac.get((s, hc - 4)),
                                  xd_evac.get((s - 1, hc + 4))]
                        tj = B.step("tensor", lambda e, hc=hc, k=k, par=par,
                                    bank=bank, half=half:
                                    e.matmul(bank[:, 0:256],
                                             DTR[:, k, hc * 128:(hc + 1) * 128],
                                             XBT[:, k, half],
                                             start=(k == 0), stop=(k == HC - 1),
                                             skip_group_check=True),
                                    inc=("e_t" if k == HC - 1 else None),
                                    waits=wx)
                        if k == HC - 1:
                            last_xd = tj
                            slot = (s * HC + hc) % 2
                            tev = B.step(
                                "scalar", lambda e, slot=slot, bank=bank:
                                e.copy(XDS[slot], bank[:, 0:256]),
                                inc="e_s", waits=[tj, xds_free[slot]])
                            xd_evac[(s, hc)] = tev
                            xds_free[slot] = B.dma(
                                "scalar",
                                xdb[:, hc * T + s * SEG:hc * T + (s + 1) * SEG],
                                XDS[slot], "d_xo" + str(slot), waits=[tev])
                            B.xd_done = tev
                            B.xd_store = list(xds_free)
                    if s + 1 < NSEG and t >= 128 and t < 128 + 8 * 8 \
                            and (t - 128) % 8 == 0:
                        emit_cast(s + 1, (t - 128) // 8)
                    tact = B.step("scalar", lambda e, t=t, g=g, par=par: e.activation(
                        STT[:, g + 1, :],
                        P[par][:, :].rearrange("p (h t) -> p t h", h=2)[:, t, :],
                        AF.Tanh),
                        inc="sca", waits=[("scm", g + 1)], chain=False)
                    if t == SEG - 1:
                        seg_free[par] = tact
                if last_xd is not None:
                    xb_done[par] = last_xd
                else:
                    xb_done[par] = tmm
                B.xd_store_seg[s] = list(xds_free)
            tok_scan = ("sca", B.cnt["sca"])

            B.p4_free = B.xd_done
            B.p5_free = B.xd_done
            xd_tok = [None, None]
            pend2 = None
            for tau in range(NT):
                tsl = slice(tau * 512, (tau + 1) * 512)
                papply = pend2() if pend2 is not None else None
                ln_collect_start()
                evac = {}
                for hc in range(HC):
                    if hc == 2 and papply is not None:
                        papply()
                        papply = None
                    par = hc % 2
                    txd = B.dma("sync", WSF[par],
                                xdb[:, hc * T + tau * 512:
                                    hc * T + (tau + 1) * 512],
                                "d_xd" + str(par),
                                waits=[xd_tok[par], B.ws_free[par],
                                       B.prepass_done[0], B.prepass_done[1]]
                                + list(B.xd_store_seg[2 * tau + 1]))
                    bank = P[par]
                    w0 = [txd, evac.get(hc - 2)]
                    if tau == 0 and hc == 0:
                        w0 += [tok_scan, tok_idb] + tok_xs + list(B.qprep_done)
                    tmm = B.step("tensor", lambda e, par=par, bank=bank:
                                 e.matmul(bank[:, :], IDB[:, :], WSF[par],
                                          start=True, stop=False,
                                          skip_group_check=True),
                                 inc=None, waits=w0)
                    for kc in range(2):
                        tmm = B.step("tensor", lambda e, kc=kc, hc=hc, bank=bank,
                                     tau=tau:
                                     e.matmul(bank[:, :],
                                              CTB[:, kc, hc * 128:(hc + 1) * 128],
                                              STT[:, 1 + tau * 512:
                                                  1 + (tau + 1) * 512, kc],
                                              start=False, stop=(kc == 1),
                                              skip_group_check=True),
                                     inc=("e_t" if kc == 1 else None),
                                     waits=([tok_ctb] if kc == 0 else []))
                    xd_tok[par] = tmm
                    B.ws_free[par] = tmm
                    tev = B.step("vector", lambda e, hc=hc, bank=bank, tsl=tsl:
                                 e.tensor_tensor(AR1[:, hc, 0:512], bank[:, :],
                                                 AR0[:, hc, tsl], op=ALU.add),
                                 inc="e_v", waits=[tmm])
                    evac[hc] = tev
                    collect_std(hc, AR1[:, hc, 0:512], tev)
                trv, trs = ln_rows()
                ty = ln_apply(AR1[:, :, 0:512], trv, trs, affine=True)
                tyc = cast_tile(XBT, AR1[:, :, 0:512],
                                [("e_t", B.cnt["e_t"])], per_hc=B.apply_hc)
                cast_hc = B.cast_hc
                gelu_toks = {}
                wtoks = {}
                for m in range(2):
                    wtoks[m] = B.dma("sync", WS[m % 3][:, :, :],
                                     r3(w1b[:, m * 128:(m + 1) * 128]),
                                     "d_w1" + str(m % 3), waits=[B.ws_free[m % 3]])
                for m in range(FC):
                    par = m % 3
                    if m + 2 < FC:
                        wtoks[m + 2] = B.dma(
                            "sync", WS[(m + 2) % 3][:, :, :],
                            r3(w1b[:, (m + 2) * 128:(m + 3) * 128]),
                            "d_w1" + str((m + 2) % 3),
                            waits=[B.ws_free[(m + 2) % 3]])
                    bank = P[m % 2]
                    tmm = None
                    for k in range(HC):
                        if m == 0:
                            w0 = ([cast_hc[0], wtoks[m]] if k == 0
                                  else [cast_hc[k]])
                        else:
                            w0 = ([wtoks[m], gelu_toks.get(m - 2)]
                                  if k == 0 else [])
                        tmm = B.step("tensor", lambda e, k=k, par=par, bank=bank:
                                     e.matmul(bank[:, :], WS[par][:, k, :],
                                              XBT[:, k, :],
                                              start=(k == 0), stop=(k == HC - 1),
                                              skip_group_check=True),
                                     inc=("e_t" if k == HC - 1 else None),
                                     waits=w0)
                    B.ws_free[par] = tmm
                    tg = B.step("scalar", lambda e, m=m, bank=bank: e.activation(
                        hm[:, m, :], bank[:, :], AF.Gelu, bias=B1C[:, m:m + 1]),
                        inc="e_s", waits=[tmm, tok_pro]
                        + list(getattr(B, "xd_store", None) or ()))
                    gelu_toks[m] = tg
                mm2_last = None
                wtoks = {}
                for k in range(2):
                    wtoks[k] = B.dma("sync", WS[k % 3][:, :, :],
                                     w2b[k * 128:(k + 1) * 128, :].rearrange(
                                         "k (a b) -> k a b", a=HC),
                                     "d_w2" + str(k % 3), waits=[B.ws_free[k % 3]])
                for k in range(FC):
                    par = k % 3
                    if k + 2 < FC:
                        wtoks[k + 2] = B.dma(
                            "sync", WS[(k + 2) % 3][:, :, :],
                            w2b[(k + 2) * 128:(k + 3) * 128, :].rearrange(
                                "k (a b) -> k a b", a=HC),
                            "d_w2" + str((k + 2) % 3),
                            waits=[B.ws_free[(k + 2) % 3]])
                    for hc in range(HC):
                        w0 = []
                        if k == 0 and hc == 0:
                            w0 = [wtoks[k], gelu_toks[FC - 1], gelu_toks[FC - 2],
                                  B.bar("e_v"), B.bar("e_s")]
                        elif hc == 0:
                            w0 = [wtoks[k]]
                        mm2_last = B.step("tensor",
                                          lambda e, k=k, hc=hc, par=par: e.matmul(
                                              P[hc][:, :], WS[par][:, hc, :],
                                              hm[:, k, :],
                                              start=(k == 0), stop=(k == FC - 1),
                                              skip_group_check=True),
                                          inc=("e_t" if hc == HC - 1 else None),
                                          waits=w0)
                    B.ws_free[par] = mm2_last
                if papply is not None:
                    papply()
                ln_collect_start()
                ev = {}

                def evac_l0a(hc, tsl=tsl):
                    B.step("vector", lambda e, hc=hc: e.scalar_tensor_tensor(
                        BCS[:, :], P[hc][:, :], B2C[:, hc:hc + 1], AR1[:, hc, 0:512],
                        op0=ALU.add, op1=ALU.add), inc="e_v",
                        waits=[mm2_last])
                    return B.step("vector", lambda e, hc=hc, tsl=tsl:
                                  e.tensor_tensor(AR0[:, hc, tsl], AR0[:, hc, tsl],
                                                  BCS[:, :], op=ALU.add),
                                  inc="e_v", waits=[])
                for hc in (6, 7):
                    ev[hc] = evac_l0a(hc)
                B.p67_free = ev[7]
                for hc in (6, 7, 0, 1, 2, 3, 4, 5):
                    if hc not in ev:
                        ev[hc] = evac_l0a(hc)
                    collect_std(hc, AR0[:, hc, tsl], ev[hc])

                def pend2(tsl=tsl, tau=tau):
                    trv, trs = ln_rows(alt=1)
                    return lambda: ln_apply(AR0[:, :, tsl], trv, trs,
                                            affine=False, alt=1, tau=tau)
            pend2()()

        def stage_l0f():
            B.l0f_bars = (B.bar("e_v"), B.bar("e_t"), B.bar("e_s"))
            wv = [l0wqb[:, hc * 128:(hc + 1) * 128].rearrange(
                "(kc k) o -> k kc o", k=128) for hc in range(HC)]
            pend = None
            for tau in range(NT):
                tsl = slice(tau * 512, (tau + 1) * 512)
                txc = cast_tile(XBT, AR0[:, :, tsl],
                                [B.bar("e_t")] + tok_xs,
                                per_hc=B.resid_hc[tau])
                cast_hc = B.cast_hc
                papply = pend() if pend is not None else None
                ln_collect_start()
                evac = {}
                wtoks = {}
                extra = list(B.qprep_done) if tau == 0 else []
                for hc in range(2):
                    wtoks[hc] = B.dma("sync", WS[hc % 3][:, :, :], wv[hc],
                                      "d_w0" + str(hc % 3),
                                      waits=[B.ws_free[hc % 3]] + extra)
                for hc in range(HC):
                    if hc == 2 and papply is not None:
                        papply()
                        papply = None
                    if hc + 2 < HC:
                        wtoks[hc + 2] = B.dma(
                            "sync", WS[(hc + 2) % 3][:, :, :], wv[hc + 2],
                            "d_w0" + str((hc + 2) % 3),
                            waits=[B.ws_free[(hc + 2) % 3]] + extra)
                    par = hc % 3
                    bank = P[hc % 2]
                    mmt = None
                    for k in range(HC):
                        if hc == 0:
                            w0 = ([cast_hc[0], wtoks[hc]] if k == 0
                                  else [cast_hc[k]])
                        else:
                            w0 = ([wtoks[hc], evac.get(hc - 2)]
                                  if k == 0 else [])
                        mmt = B.step("tensor", lambda e, k=k, par=par, bank=bank:
                                     e.matmul(bank[:, :],
                                              WS[par][:, k, :],
                                              XBT[:, k, :],
                                              start=(k == 0), stop=(k == HC - 1),
                                              skip_group_check=True),
                                     inc=("e_t" if k == HC - 1 else None),
                                     waits=w0)
                    B.ws_free[par] = mmt
                    tev = B.step("vector", lambda e, hc=hc, tsl=tsl:
                                 e.tensor_tensor(AR0[:, hc, tsl], AR0[:, hc, tsl],
                                                 P[hc % 2][:, :], op=ALU.add),
                                 inc="e_v", waits=[mmt])
                    evac[hc] = tev
                    collect_std(hc, AR0[:, hc, tsl], tev)

                def pend(tsl=tsl, tau=tau):
                    trv, trs = ln_rows(alt=1)
                    return lambda: ln_apply(AR0[:, :, tsl], trv, trs,
                                            affine=False, alt=1, tau=tau)
            pend()()
            B.l0f_ws = list(B.ws_free)

        def stage_l1a():
            bar_v = B.bar("e_v"); bar_s = B.bar("e_s")
            if "gam" not in L1A_PH:
                tlg = tal = tlgc = tac = tdec = tgw = tgc = tmsk = None
                tok_ww = tz1 = tz2 = None
                if "mem" in L1A_PH:
                    tz1 = B.step("vector", lambda e: e.memset(
                        AR1[:, :, :].rearrange("p k o -> p (k o)"), 0.0),
                        inc="e_v", waits=[bar_s])
                    tz2 = B.step("vector", lambda e: e.memset(
                        WSTB[:, :, :].rearrange("p k o -> p (k o)"), 0.0),
                        inc="e_v", waits=[])
                if "ww" in L1A_PH:
                    tok_ww = B.dma("gpsimd", WWB[:, :, :], r3(d["wwT"]), "d_ww")
                return _l1a_body(tlg, tal, tlgc, tac, tdec, tgw, tgc, tmsk,
                                 tok_ww, tz1, tz2)
            tg = B.step("scalar", lambda e: e.activation(
                CELLS[0:1, 4:5], CELLS[0:1, 0:1], AF.Sigmoid), inc="e_s",
                waits=[tok_pro])
            tlg = B.step("scalar", lambda e: e.activation(
                CELLS[0:1, 5:6], CELLS[0:1, 4:5], AF.Ln), inc="e_s", waits=[])
            tal = B.step("scalar", lambda e: e.activation(
                CELLS[0:1, 6:7], CELLS[0:1, 1:2], AF.Exp), inc="e_s", waits=[])
            tb1 = B.step("tensor", lambda e: e.matmul(
                P[6][:, 0:1], ONFR[0:1, :], CELLS[0:1, 5:6], start=True, stop=True,
                skip_group_check=True), inc=None,
                waits=[tlg, tok_pro, B.p67_free, bar_v])
            tb2 = B.step("tensor", lambda e: e.matmul(
                P[6][:, 1:2], ONFR[0:1, :], CELLS[0:1, 6:7], start=False, stop=True,
                skip_group_check=True), inc="e_t", waits=[tal])
            tlgc = B.step("vector", lambda e: e.tensor_copy(COLS[:, 4:5], P[6][:, 0:1]),
                          inc="e_v", waits=[tb2])
            tac = B.step("vector", lambda e: e.tensor_copy(COLS[:, 3:4], P[6][:, 1:2]),
                         inc="e_v", waits=[])
            tv = B.step("vector", lambda e: e.tensor_scalar(
                COLS[:, 0:1], ARNG[:, :], COLS[:, 4:5], 0.0, op0=ALU.mult,
                op1=ALU.add), inc="e_v", waits=[tlgc])
            tdec = B.step("scalar", lambda e: e.activation(
                COLS[:, 0:1], COLS[:, 0:1], AF.Exp), inc="e_s", waits=[tv])
            tv = B.step("vector", lambda e: e.tensor_scalar(
                COLS[:, 1:2], REVC[:, :], COLS[:, 4:5], 0.0, op0=ALU.mult,
                op1=ALU.add), inc="e_v", waits=[tlgc])
            tgw = B.step("scalar", lambda e: e.activation(
                COLS[:, 1:2], COLS[:, 1:2], AF.Exp), inc="e_s", waits=[tv])
            tv = B.step("vector", lambda e: e.tensor_scalar(
                COLS[:, 2:3], COLS[:, 4:5], float(CC), 0.0, op0=ALU.mult,
                op1=ALU.add), inc="e_v", waits=[tlgc])
            tgc = B.step("scalar", lambda e: e.activation(
                COLS[:, 2:3], COLS[:, 2:3], AF.Exp), inc="e_s", waits=[tv])
            tv = B.step("vector", lambda e: e.tensor_scalar(
                MSK[:, :], MSK[:, :], COLS[:, 4:5], 0.0, op0=ALU.mult, op1=ALU.add),
                inc="e_v", waits=[tlgc, tok_pro])
            tmsk = B.step("scalar", lambda e: e.activation(
                MSK[:, :], MSK[:, :], AF.Exp), inc="e_s", waits=[tv])
            tok_ww = B.dma("gpsimd", WWB[:, :, :], r3(d["wwT"]), "d_ww")
            tz1 = B.step("vector", lambda e: e.memset(
                AR1[:, :, :].rearrange("p k o -> p (k o)"), 0.0), inc="e_v",
                waits=[bar_s])
            tz2 = B.step("vector", lambda e: e.memset(
                WSTB[:, :, :].rearrange("p k o -> p (k o)"), 0.0), inc="e_v",
                waits=[B.bar("e_t")])
            return _l1a_body(tlg, tal, tlgc, tac, tdec, tgw, tgc, tmsk,
                             tok_ww, tz1, tz2)

        def _l1a_body(tlg, tal, tlgc, tac, tdec, tgw, tgc, tmsk,
                      tok_ww, tz1, tz2):
            tok_wstb_prev = tz2
            tok_x2c_free = [B.bar("e_t"), B.bar("e_t")]
            tok_p01_free = B.bar("e_v")
            tok_p23_free = B.bar("e_v")
            tok_upd_free = [None, None]
            tok_tr_free = [None, None, None, None]
            last_rd_copy = None

            def rk(k, b):
                return x2c[:, b, k, 1:129]

            def wk(k, b):
                return x2c[:, b, k, 0:128]

            p6b = P[6][:, :].bitcast(BF16)
            p7b = P[7][:, :].bitcast(BF16)

            for c in range(NCH):
                b = c % 2
                if "cast" not in L1A_PH:
                    tcast = None
                elif c == 0:
                    tzc = B.step("vector", lambda e, b=b: e.memset(
                        x2c[:, b, :, 0:1].rearrange("p k c -> p (k c)"), 0.0),
                        inc="e_v", waits=tok_xs + [tok_x2c_free[b]])
                    cast_k = []
                    for k in range(HC):
                        tcast = B.step("vector", lambda e, k=k, b=b: e.tensor_copy(
                            x2c[:, b, k, 1:129], AR0[:, k, 0:128]),
                            inc="e_v", waits=([tzc] if k == 0 else []))
                        cast_k.append(tcast)
                else:
                    cast_k = []
                    for k in range(HC):
                        tcast = B.step("vector", lambda e, k=k, b=b, c=c:
                                       e.tensor_copy(
                                           x2c[:, b, k, 0:129],
                                           AR0[:, k, c * CC - 1:c * CC + 128]),
                                       inc="e_v",
                                       waits=(tok_xs + [tok_x2c_free[b]]
                                              if k == 0 else []))
                        cast_k.append(tcast)
                tmm = None
                if "v" in L1A_PH:
                 for dh in range(2):
                    for k in range(HC):
                        if dh == 0:
                            w0 = [cast_k[k]] + ([tok_ww, tok_p01_free]
                                                if k == 0 else [])
                        else:
                            w0 = []
                        tmm = B.step("tensor", lambda e, dh=dh, k=k, b=b: e.matmul(
                            P[dh][:, :], rk(k, b),
                            WWB[:, k, dh * 512:(dh + 1) * 512],
                            start=(k == 0), stop=(k == HC - 1),
                            skip_group_check=True),
                            inc=("e_t" if (dh == 1 and k == HC - 1) else None),
                            waits=w0)
                B.last_v = tmm
                tut = None
                tvt = None
                tok_p01_free_v = None
                if "evv" in L1A_PH:
                 for dh in range(2):
                    tut = B.step("vector", lambda e, dh=dh: e.tensor_scalar(
                        utm[:, dh, :], P[dh][:, :], COLS[:, 1:2], 0.0,
                        op0=ALU.mult, op1=ALU.add), inc="e_v",
                        waits=([tmm, tgw] if dh == 0 else []))
                 for dh in range(2):
                    tvt = B.step("vector", lambda e, dh=dh: e.tensor_copy(
                        vtm[:, dh, :], P[dh][:, :]), inc="e_v", waits=[])
                 tok_p01_free_v = tvt
                tsm = None
                if "S" in L1A_PH:
                 for k in range(HC):
                    tsm = B.step("tensor", lambda e, k=k, b=b: e.matmul(
                        P[6][:, 0:128], rk(k, b), wk(k, b),
                        start=(k == 0), stop=(k == HC - 1), skip_group_check=True),
                        inc=("e_t" if k == HC - 1 else None),
                        waits=([cast_k[0], B.p67_free, tok_tr_free[0],
                                tok_tr_free[2], tok_tr_free[1], tok_tr_free[3]]
                               if k == 0 else []))
                tsmtb = None
                if "smt" in L1A_PH:
                 tsmb = B.step("vector", lambda e: e.tensor_tensor(
                    SMB[:, :], P[6][:, 0:128], MSK[:, :], op=ALU.mult),
                    inc="e_v", waits=[tsm, tmsk])
                 tsmt = B.step("tensor", lambda e: e.transpose(
                    p6b[:, 256:384], SMB[:, :], IDB[:, :]), inc="e_t",
                    waits=[tsmb, tok_idb])
                 tsmtb = B.step("vector", lambda e: e.tensor_copy(
                    SMTB[:, :], p6b[:, 256:384]), inc="e_v", waits=[tsmt])
                 B.p67_free = tsmtb
                tint = None
                if "inter" in L1A_PH:
                 for dh in range(2):
                    for k in range(HC):
                        tint = B.step("tensor", lambda e, dh=dh, k=k, b=b: e.matmul(
                            P[2 + dh][:, :], rk(k, b),
                            WSTB[:, k, dh * 512:(dh + 1) * 512],
                            start=(k == 0), stop=(k == HC - 1),
                            skip_group_check=True),
                            inc=("e_t" if (dh == 1 and k == HC - 1) else None),
                            waits=([cast_k[0], tok_wstb_prev, tok_p23_free]
                                   if (dh == 0 and k == 0) else []))
                tdecs = None
                if "dec" in L1A_PH:
                 for dh in range(2):
                    tdecs = B.step("vector", lambda e, dh=dh: e.tensor_scalar(
                        P[2 + dh][:, :], P[2 + dh][:, :], COLS[:, 0:1], 0.0,
                        op0=ALU.mult, op1=ALU.add), inc="e_v",
                        waits=([tint, tdec] if dh == 0 else []))
                tintra = None
                if "intra" in L1A_PH:
                 for dh in range(2):
                    tintra = B.step("tensor", lambda e, dh=dh: e.matmul(
                        P[2 + dh][:, :], SMTB[:, :], vtm[:, dh, :],
                        start=False, stop=True, skip_group_check=True),
                        inc=("e_t" if dh == 1 else None),
                        waits=([tdecs, tsmtb, tvt] if dh == 0 else []))
                trt = None
                if "rdev" in L1A_PH:
                 for dh in range(2):
                    trt = B.step("vector", lambda e, dh=dh: e.tensor_copy(
                        rtm.rearrange("p (a c) -> p a c", a=2)[:, dh, :],
                        P[2 + dh][:, :]), inc="e_v",
                        waits=([tintra] if dh == 0 else []))
                 tok_p23_free = trt
                def tr_slot(i):
                    bank = P[6 + (i % 2)][:, :].bitcast(BF16)
                    sub = (i // 2) % 2
                    return bank[:, sub * 128:sub * 128 + 128], i % 4

                if "rdtr" in L1A_PH:
                 for i in range(RDTR_N):
                    pslot, si = tr_slot(i)
                    ttr = B.step("tensor", lambda e, i=i, pslot=pslot: e.transpose(
                        pslot, rtm[:, i * 128:(i + 1) * 128], IDB[:, :]),
                        inc="e_t",
                        waits=[trt, tok_tr_free[si], tok_idb, B.p67_free])
                    if RDTR_COPY:
                        dst = reads[:, i, c * CC:(c + 1) * CC]
                        tcp = B.step("vector", lambda e, dst=dst, pslot=pslot:
                                     e.tensor_copy(dst, pslot),
                                     inc="e_v", waits=[ttr])
                        tok_tr_free[si] = tcp
                        last_rd_copy = tcp
                twk = None
                if "wktr" in L1A_PH:
                 for i in range(HC):
                    pslot, si = tr_slot(i)
                    ttr = B.step("tensor", lambda e, i=i, b=b, pslot=pslot:
                                 e.transpose(pslot, wk(i, b), IDB[:, :]),
                                 inc="e_t", waits=[tok_tr_free[si], tok_idb])
                    twk = B.step("vector", lambda e, i=i, pslot=pslot: e.tensor_copy(
                        wkt[:, i * 128:(i + 1) * 128], pslot),
                        inc="e_v", waits=[ttr])
                    tok_tr_free[si] = twk
                tok_wstb = None
                if "upd" in L1A_PH:
                 for m in range(HC):
                    par = m % 2
                    banks = (P[4], P[5]) if par == 0 else (P[0], P[1])
                    if par == 0:
                        wpar = [tok_upd_free[0], B.p4_free, B.p5_free]
                    else:
                        wpar = [tok_upd_free[1], tok_p01_free_v, tut]
                    tup = None
                    for dh in range(2):
                        tup = B.step("tensor", lambda e, m=m, dh=dh, banks=banks:
                                     e.matmul(banks[dh][:, :],
                                              wkt[:, m * 128:(m + 1) * 128],
                                              utm[:, dh, :],
                                              start=True, stop=True,
                                              skip_group_check=True),
                                     inc=("e_t" if dh == 1 else None),
                                     waits=([twk, tut] + wpar if dh == 0 else []))
                    tsv = None
                    for dh in range(2):
                        tsv = B.step("vector", lambda e, m=m, dh=dh, banks=banks:
                                     e.scalar_tensor_tensor(
                                         AR1[:, m, dh * 512:(dh + 1) * 512],
                                         AR1[:, m, dh * 512:(dh + 1) * 512],
                                         COLS[:, 2:3], banks[dh][:, :],
                                         op0=ALU.mult, op1=ALU.add),
                                     inc="e_v",
                                     waits=([tup, tgc, tz1] if dh == 0 else []))
                    if par == 0:
                        B.p4_free = tsv
                        B.p5_free = tsv
                    tok_upd_free[par] = tsv
                    tok_wstb = B.step("scalar", lambda e, m=m: e.activation(
                        WSTB[:, m, :], AR1[:, m, :], AF.Identity), inc="e_s",
                        waits=[tsv, tz2])
                if tok_wstb is not None:
                    tok_wstb_prev = tok_wstb
                tok_p01_free = tok_upd_free[1] or tok_p01_free_v or tok_p01_free
                tok_x2c_free[b] = twk or B.bar("e_t")
            if not L1A_FIN:
                return
            tok_wr = B.dma("gpsimd", WWB[:, :, :], r3(d["wrT"]), "d_wr",
                           waits=[getattr(B, "last_v", None) or B.bar("e_t")])
            bar_t = B.bar("e_t"); bar_v2 = B.bar("e_v"); bar_s2 = B.bar("e_s")
            pend = None
            for tau in range(NT):
                tsl = slice(tau * 512, (tau + 1) * 512)
                papply = pend() if pend is not None else None
                ln_collect_start()
                evac = {}
                for hc in range(HC):
                    if hc == 2 and papply is not None:
                        papply()
                        papply = None
                    bank = P[hc % 2]
                    mmt = None
                    for k in range(HC):
                        mmt = B.step("tensor", lambda e, k=k, hc=hc, bank=bank,
                                     tau=tau:
                                     e.matmul(bank[:, :],
                                              WWB[:, k, hc * 128:(hc + 1) * 128],
                                              reads[:, k, tau * 512:(tau + 1) * 512],
                                              start=(k == 0), stop=(k == HC - 1),
                                              skip_group_check=True),
                                     inc=("e_t" if k == HC - 1 else None),
                                     waits=([tok_wr, last_rd_copy, bar_v2, bar_s2,
                                             evac.get(hc - 2)] if k == 0 else []))
                    t1 = B.step("vector", lambda e, hc=hc, tsl=tsl:
                                e.scalar_tensor_tensor(T1F[:, :], P[hc % 2][:, :],
                                                       COLS[:, 3:4],
                                                       AR0[:, hc, tsl],
                                                       op0=ALU.mult, op1=ALU.add),
                                inc="e_v", waits=[mmt, tac])
                    t2 = B.step("vector", lambda e, hc=hc, tsl=tsl: e.tensor_tensor(
                        AR0[:, hc, tsl], AR0[:, hc, tsl], T1F[:, :], op=ALU.add),
                        inc="e_v", waits=[])
                    evac[hc] = t2
                    collect_std(hc, AR0[:, hc, tsl], t2)

                def pend(tsl=tsl, tau=tau):
                    trv, trs = ln_rows(alt=1)
                    return lambda: ln_apply(AR0[:, :, tsl], trv, trs,
                                            affine=False, alt=1, tau=tau)
            pend()()

        QBUF = [AR1[:, 0:2, :].rearrange("p a b -> p (a b)"),
                AR1[:, 2:4, :].rearrange("p a b -> p (a b)")]

        QSLOT = {0: 11, 1: 13, 2: 12}

        def _qviews(wi, bb):
            if wi in (0, 2):
                return QBUF[bb].rearrange("p (kc f) -> p kc f", kc=HC)
            return QBUF[bb].rearrange("p (kc o) -> p kc o", kc=2)

        def quant_prep_pass1():
            B.qp = {}
            prev = [None, None]
            for wi, wname in [(0, "wfcT"), (1, "wprojT"), (2, "l0wT")]:
                slot = QSLOT[wi]
                redop = ALU.add if wi == 2 else ALU.max
                tz = B.step("vector", lambda e, slot=slot: e.memset(
                    COLS[:, slot:slot + 1], 0.0), inc="e_v", waits=[])
                if wi == 0:
                    dv = d[wname].rearrange("(kc k) f -> k kc f", k=128)
                    chunks = [dv[:, :, j * 256:(j + 1) * 256] for j in range(16)]
                elif wi == 2:
                    dv = d[wname].rearrange("(kc k) o -> k kc o", k=128)
                    chunks = [dv[:, :, j * 256:(j + 1) * 256] for j in range(4)]
                else:
                    dv = d[wname].rearrange("(kc k) o -> k kc o", k=128)
                    chunks = [dv[:, j * 2:(j + 1) * 2, :] for j in range(16)]
                vtk = [None, None]
                for j, ch in enumerate(chunks):
                    bb = j % 2
                    tld = B.dma("sync", _qviews(wi, bb), ch, f"d_q{wi}{bb}",
                                waits=[vtk[bb], prev[bb]]
                                + ([tok_x] if wi == 0 and j < 2 else []))
                    tr = B.step("vector", lambda e, bb=bb, redop=redop:
                                e.tensor_reduce(
                                    COLS[:, 15:16], QBUF[bb], axis=AX, op=redop,
                                    apply_absolute_value=True), inc="e_v",
                                waits=[tld, tz])
                    vtk[bb] = B.step("vector", lambda e, slot=slot, redop=redop:
                                     e.tensor_tensor(
                                         COLS[:, slot:slot + 1],
                                         COLS[:, slot:slot + 1],
                                         COLS[:, 15:16], op=redop),
                                     inc="e_v", waits=[tr])
                B.qp[wi] = vtk
                prev = vtk

        QS_SR = {0: (7, 8), 1: (9, 10), 2: (5, 6)}

        def quant_prep_pe_and_pass2():
            otk_all = []
            prev2 = list(B.qp[2])
            prev2o = [None, None]
            for wi in (0, 1, 2):
                vtk = B.qp[wi]
                slot_s, slot_r = QS_SR[wi]
                ttr = B.step("tensor", lambda e, wi=wi: e.transpose(
                    P[6][0:1, 0:128], COLS[:, QSLOT[wi]:QSLOT[wi] + 1], IDF[:, :]),
                    inc="e_t", waits=[vtk[0], vtk[1], tok_pro, B.p67_free])
                if wi == 2:
                    tmx = B.step("vector", lambda e: e.tensor_reduce(
                        CELLS[0:1, 8:9], P[6][0:1, 0:128], axis=AX, op=ALU.add),
                        inc="e_v", waits=[ttr])
                    tsc = B.step("vector", lambda e: e.tensor_scalar(
                        CELLS[0:1, 8:9], CELLS[0:1, 8:9], 1.0 / float(H * H),
                        1e-5, op0=ALU.mult, op1=ALU.max), inc="e_v", waits=[tmx])
                else:
                    tmx = B.step("vector", lambda e: e.tensor_reduce(
                        CELLS[0:1, 8:9], P[6][0:1, 0:128], axis=AX, op=ALU.max),
                        inc="e_v", waits=[ttr])
                    tsc = B.step("vector", lambda e: e.tensor_scalar(
                        CELLS[0:1, 8:9], CELLS[0:1, 8:9], 1.0 / 127.0, 0.0,
                        op0=ALU.mult, op1=ALU.add), inc="e_v", waits=[tmx])
                tri = B.step("vector", lambda e: e.reciprocal(
                    CELLS[0:1, 9:10], CELLS[0:1, 8:9]), inc="e_v", waits=[tsc])
                tb1 = B.step("tensor", lambda e: e.matmul(
                    P[6][:, 0:1], ONFR[0:1, :], CELLS[0:1, 8:9], start=True,
                    stop=True, skip_group_check=True), inc=None,
                    waits=[tri, tmx])
                tb2 = B.step("tensor", lambda e: e.matmul(
                    P[6][:, 1:2], ONFR[0:1, :], CELLS[0:1, 9:10], start=False,
                    stop=True, skip_group_check=True), inc="e_t", waits=[])
                tc1 = B.step("vector", lambda e, slot_s=slot_s: e.tensor_copy(
                    COLS[:, slot_s:slot_s + 1], P[6][:, 0:1]), inc="e_v",
                    waits=[tb2])
                tc2 = B.step("vector", lambda e, slot_r=slot_r: e.tensor_copy(
                    COLS[:, slot_r:slot_r + 1], P[6][:, 1:2]), inc="e_v", waits=[])
                B.p67_free = tc2
                B.qtc = getattr(B, "qtc", {})
                B.qtc[wi] = (tc1, tc2)
            for wi, wq, nch, lo, hi in [(2, l0wqb, 4, -1.0, 1.0),
                                        (0, wqfc, 16, -128.0, 127.0),
                                        (1, wqproj, 16, -128.0, 127.0)]:
                slot_s, slot_r = QS_SR[wi]
                tc1, tc2 = B.qtc[wi]
                if wi == 0:
                    ovv = wq.rearrange("(kc k) f -> k kc f", k=128)
                    ov = [ovv[:, :, j * 256:(j + 1) * 256] for j in range(nch)]
                    sv = d["wfcT"].rearrange("(kc k) f -> k kc f", k=128)
                    cv = [sv[:, :, j * 256:(j + 1) * 256] for j in range(nch)]
                elif wi == 2:
                    ovv = wq.rearrange("(kc k) o -> k kc o", k=128)
                    ov = [ovv[:, :, j * 256:(j + 1) * 256] for j in range(nch)]
                    sv = d["l0wT"].rearrange("(kc k) o -> k kc o", k=128)
                    cv = [sv[:, :, j * 256:(j + 1) * 256] for j in range(nch)]
                else:
                    ovv = wq.rearrange("(kc k) o -> k kc o", k=128)
                    ov = [ovv[:, j * 2:(j + 1) * 2, :] for j in range(nch)]
                    sv = d["wprojT"].rearrange("(kc k) o -> k kc o", k=128)
                    cv = [sv[:, j * 2:(j + 1) * 2, :] for j in range(nch)]
                wpv = [wpq[:, 0, :], wpq[:, 1, :]]
                wpv3 = [_qv_out(wi, bb) for bb in range(2)]
                vtk2 = [None, None]
                otk = [None, None]
                for j in range(nch):
                    bb = j % 2
                    buf = QBUF[bb]
                    tld = B.dma("sync", _qviews(wi, bb), cv[j], f"d_q{wi}{bb}",
                                waits=[vtk2[bb], otk[bb], prev2[bb]])
                    vtk2[bb] = emit_round(buf, wpv[bb],
                                          COLS[:, slot_r:slot_r + 1],
                                          COLS[:, slot_s:slot_s + 1],
                                          lo, hi,
                                          [tld, tc1, tc2, otk[bb], prev2o[bb]])
                    otk[bb] = B.dma("sync", ov[j], wpv3[bb], f"d_qo{wi}{bb}",
                                    waits=[vtk2[bb]])
                otk_all += otk
                prev2 = list(vtk2)
                prev2o = list(otk)
            B.qprep_done = tuple(otk_all)

        def _qv_out(wi, bb):
            if wi in (0, 2):
                return wpq[:, bb, :].rearrange("p (kc f) -> p kc f", kc=HC)
            return wpq[:, bb, :].rearrange("p (kc o) -> p kc o", kc=2)

        def stage_l1f():
            tok_apj = B.dma("gpsimd", APJT[:, :, :], r3(d["aprojT"]), "d_apj",
                            waits=[B.bar("e_t"), B.bar("sca")])
            tok_bpjt = B.dma("gpsimd", BPJT[:, :], d["bprojT"][:, :], "d_bpjt")
            wq_fcv = wqfc.rearrange("(kc k) f -> k kc f", k=128)
            wq_pjv = wqproj.rearrange("(kc k) o -> k kc o", k=128)
            lw = getattr(B, "l0f_ws", None)
            B.ws_free = ([lw[0], lw[1], B.bar("e_t")] if lw
                         else [B.bar("e_t")] * 3)
            pend = None
            for tau in range(NT):
                tsl = slice(tau * 512, (tau + 1) * 512)
                txc = cast_tile(XBT, AR0[:, :, tsl],
                                [B.bar("e_t")] + tok_xs,
                                per_hc=B.resid_hc[tau])
                cast_hc = B.cast_hc
                papply = pend() if pend is not None else None
                txa = None
                for k in range(HC):
                    txa = B.step("tensor", lambda e, k=k: e.matmul(
                        P[6][0:16, :], AFCT[:, k, :], XBT[:, k, :],
                        start=(k == 0), stop=(k == HC - 1), skip_group_check=True),
                        inc=("e_t" if k == HC - 1 else None),
                        waits=([cast_hc[0], tok_afc, B.p67_free] if k == 0
                               else [cast_hc[k]]))
                txab = B.step("vector", lambda e: e.tensor_scalar(
                    XAB[:, :], P[6][0:16, :], 2.0, 0.0, op0=ALU.mult, op1=ALU.add),
                    inc="e_v", waits=[txa])
                B.p67_free = txab
                gelu_toks = {}
                tha = None
                wtoks = {}
                for m in range(2):
                    wtoks[m] = B.dma("sync", WS[m % 3][:, :, :],
                                     wq_fcv[:, :, m * 128:(m + 1) * 128],
                                     "d_lf" + str(m % 3),
                                     waits=[B.ws_free[m % 3]] + list(B.qprep_done))
                for m in range(FC):
                    par = m % 3
                    if m + 2 < FC:
                        wtoks[m + 2] = B.dma(
                            "sync", WS[(m + 2) % 3][:, :, :],
                            wq_fcv[:, :, (m + 2) * 128:(m + 3) * 128],
                            "d_lf" + str((m + 2) % 3),
                            waits=[B.ws_free[(m + 2) % 3]])
                    bank = P[m % 2]
                    tmm = None
                    for k in range(HC):
                        tmm = B.step("tensor", lambda e, k=k, par=par, bank=bank:
                                     e.matmul(bank[:, :], WS[par][:, k, :],
                                              XBT[:, k, :],
                                              start=(k == 0), stop=False,
                                              skip_group_check=True),
                                     inc=None,
                                     waits=([wtoks[m], gelu_toks.get(m - 2)]
                                            if k == 0 else []))
                    tmm = B.step("tensor", lambda e, m=m, bank=bank: e.matmul(
                        bank[:, :], BFCT[:, m * 128:(m + 1) * 128], XAB[:, :],
                        start=False, stop=True, skip_group_check=True),
                        inc="e_t", waits=[txab, tok_bfct])
                    B.ws_free[par] = tmm
                    tg = B.step("scalar", lambda e, m=m, bank=bank: e.activation(
                        hm[:, m, :], bank[:, :], AF.Gelu, bias=BFCc[:, m:m + 1]),
                        inc="e_s", waits=[tmm, tok_pro])
                    if m > 0:
                        tha = B.step("tensor", lambda e, m=m: e.matmul(
                            P[6][0:16, :], APJT[:, m - 1, :], hm[:, m - 1, :],
                            start=(m == 1), stop=False, skip_group_check=True),
                            inc=None, waits=[gelu_toks[m - 1], tok_apj])
                    gelu_toks[m] = tg
                tha = B.step("tensor", lambda e: e.matmul(
                    P[6][0:16, :], APJT[:, FC - 1, :], hm[:, FC - 1, :],
                    start=False, stop=True, skip_group_check=True),
                    inc="e_t", waits=[gelu_toks[FC - 1]])
                thab = B.step("vector", lambda e: e.tensor_scalar(
                    HAB[:, :], P[6][0:16, :], 2.0, 0.0, op0=ALU.mult, op1=ALU.add),
                    inc="e_v", waits=[tha])
                B.p67_free = thab
                if papply is not None:
                    papply()
                    papply = None
                mm2_last = None
                wtoks = {}
                for k in range(2):
                    wtoks[k] = B.dma("sync", WS[k % 3][:, :, :],
                                     wq_pjv[:, k, :].rearrange(
                                         "p (a b) -> p a b", a=HC),
                                     "d_lg" + str(k % 3),
                                     waits=[B.ws_free[k % 3]])
                for k in range(FC):
                    par = k % 3
                    if k + 2 < FC:
                        wtoks[k + 2] = B.dma(
                            "sync", WS[(k + 2) % 3][:, :, :],
                            wq_pjv[:, k + 2, :].rearrange(
                                "p (a b) -> p a b", a=HC),
                            "d_lg" + str((k + 2) % 3),
                            waits=[B.ws_free[(k + 2) % 3]])
                    for hc in range(HC):
                        w0 = []
                        if k == 0 and hc == 0:
                            w0 = [wtoks[k], gelu_toks[FC - 1], gelu_toks[FC - 2],
                                  thab, B.bar("e_v"), B.bar("e_s")]
                        elif hc == 0:
                            w0 = [wtoks[k]]
                        mm2_last = B.step("tensor",
                                          lambda e, k=k, hc=hc, par=par: e.matmul(
                                              P[hc][:, :], WS[par][:, hc, :],
                                              hm[:, k, :],
                                              start=(k == 0), stop=False,
                                              skip_group_check=True),
                                          inc=("e_t" if hc == HC - 1 else None),
                                          waits=w0)
                    B.ws_free[par] = mm2_last
                for hc in range(HC):
                    mm2_last = B.step("tensor", lambda e, hc=hc: e.matmul(
                        P[hc][:, :], BPJT[:, hc * 128:(hc + 1) * 128], HAB[:, :],
                        start=False, stop=True, skip_group_check=True),
                        inc=("e_t" if hc == HC - 1 else None),
                        waits=([tok_bpjt] if hc == 0 else []))
                ln_collect_start()
                ev = {}

                def evac_l1f(hc, tsl=tsl):
                    return B.step("vector", lambda e, hc=hc, tsl=tsl:
                                  e.scalar_tensor_tensor(
                                      AR0[:, hc, tsl], P[hc][:, :],
                                      BPJ[:, hc:hc + 1], AR0[:, hc, tsl],
                                      op0=ALU.add, op1=ALU.add),
                                  inc="e_v", waits=[mm2_last])
                for hc in (6, 7):
                    ev[hc] = evac_l1f(hc)
                B.p67_free = ev[7]
                for hc in (6, 7, 0, 1, 2, 3, 4, 5):
                    if hc not in ev:
                        ev[hc] = evac_l1f(hc)
                    collect_std(hc, AR0[:, hc, tsl], ev[hc])

                def pend(tsl=tsl, tau=tau):
                    trv, trs = ln_rows(alt=1)
                    return lambda: ln_apply(AR0[:, :, tsl], trv, trs,
                                            affine=False, alt=1, tau=tau)
            pend()()

        def epilogue():
            yv = y.rearrange("(kc k) t -> k kc t", k=128)
            tok_dma = None
            for tau in range(NT):
                tsl = slice(tau * 512, (tau + 1) * 512)
                w = list(B.resid_hc[tau]) if B.resid_hc.get(tau) \
                    else [B.bar("e_v")]
                tok_dma = B.dma("gpsimd", yv[:, :, tsl], AR0[:, :, tsl],
                                "d_out", waits=w)
            B.step("gpsimd", lambda e: e.engine_nop(), inc="e_g",
                   waits=[tok_dma])

        fused_prep = "l0a" in stages and "l1f" in stages
        B.prepass_done = (None, None)
        if "l0a" in stages:
            weight_prepass()
            if fused_prep:
                quant_prep_pass1()
                stage_l0a(mid_hook=quant_prep_pe_and_pass2)
            else:
                stage_l0a()
        if "l0f" in stages:
            stage_l0f()
        if "l1a" in stages:
            stage_l1a()
        if "l1f" in stages:
            if not fused_prep:
                quant_prep_pass1()
                quant_prep_pe_and_pass2()
            stage_l1f()
        epilogue()

        with nc.Block() as block:
            B.replay(block)
    return nc



def _prep_inputs(inputs):
    f = np.float32
    ac = np.ascontiguousarray
    bb = ml_dtypes.bfloat16
    pr = {}
    pr["bmT"] = ac(np.asarray(inputs["L0_attn_Bm"], f).T)
    pr["aT"] = ac(np.asarray(inputs["L0_attn_A"], f).T)
    pr["cT"] = ac(np.asarray(inputs["L0_attn_C"], f).T)
    pr["dT"] = ac(np.asarray(inputs["L0_attn_D"], f).T)
    pr["lnw_c"] = ac(np.asarray(inputs["L0_attn_ln_w"], f).reshape(HC, 128).T)
    pr["lnb_c"] = ac(np.asarray(inputs["L0_attn_ln_b"], f).reshape(HC, 128).T)
    pr["w1T"] = ac(np.asarray(inputs["L0_attn_W1"], f).T)
    pr["b1_c"] = ac(np.asarray(inputs["L0_attn_b1"], f).reshape(FC, 128).T)
    pr["w2T"] = ac(np.asarray(inputs["L0_attn_W2"], f).T)
    pr["b2_c"] = ac(np.asarray(inputs["L0_attn_b2"], f).reshape(HC, 128).T)
    pr["l0wT"] = ac(np.asarray(inputs["L0_ffn_weight"], f).T)
    pr["wwT"] = ac(np.asarray(inputs["L1_attn_W_write"], f).T)
    pr["wrT"] = ac(np.asarray(inputs["L1_attn_W_read"], f).T)
    pr["decay"] = np.asarray(inputs["L1_attn_decay"], f).reshape(1, 1)
    pr["logalpha"] = np.asarray(inputs["L1_attn_log_alpha"], f).reshape(1, 1)
    pr["wfcT"] = ac(np.asarray(inputs["L1_ffn_W_fc"], f).T)
    pr["bfc_c"] = ac(np.asarray(inputs["L1_ffn_b_fc"], f).reshape(FC, 128).T)
    pr["afcT"] = ac(np.asarray(inputs["L1_ffn_A_fc"], f).T)
    pr["bfcT"] = ac(np.asarray(inputs["L1_ffn_B_fc"], f).T)
    pr["wprojT"] = ac(np.asarray(inputs["L1_ffn_W_proj"], f).T)
    pr["bprj_c"] = ac(np.asarray(inputs["L1_ffn_b_proj"], f).reshape(HC, 128).T)
    pr["aprojT"] = ac(np.asarray(inputs["L1_ffn_A_proj"], f).T)
    pr["bprojT"] = ac(np.asarray(inputs["L1_ffn_B_proj"], f).T)
    pr["identb"] = np.eye(128, dtype=f).astype(bb)
    pr["identf"] = np.eye(128, dtype=f)
    pr["onesb_col"] = np.ones((128, 1), f).astype(bb)
    pr["onesf_row"] = np.ones((1, 128), f)
    pr["arange"] = np.arange(128, dtype=f).reshape(128, 1)
    pr["rev"] = (CC - 1.0 - np.arange(128, dtype=f)).reshape(128, 1)
    p = np.arange(CC)
    diffs = np.maximum(p[:, None] - 1 - p[None, :], 0).astype(f)
    diffs[p[:, None] <= p[None, :]] = 1e9
    pr["diffs"] = ac(diffs)
    return pr


def kernel(**inputs):
    x = np.asarray(inputs["L0_attn_x"], np.float32)
    pr = _prep_inputs(inputs)
    ncores = 8
    xTs = [np.ascontiguousarray(x[c].T) for c in range(ncores)]

    stream = None
    out = None
    for gi, stages in enumerate(GROUPS):
        first = gi == 0
        last = gi == len(GROUPS) - 1
        nc = build_group(stages, first, last)
        in_maps = []
        for c in range(ncores):
            m = dict(pr)
            m["xT"] = xTs[c]
            if not first:
                m["sin"] = stream[c]
            in_maps.append(m)
        res = run_bass_kernel_spmd(nc, in_maps, core_ids=list(range(ncores)))
        if last:
            out = np.stack([res.results[c]["y"].T for c in range(ncores)],
                           axis=0)
        else:
            stream = [np.ascontiguousarray(res.results[c]["sout"])
                      for c in range(ncores)]
    return out
